# version 03
# baseline (speedup 1.0000x reference)
import contextlib
import numpy as np
import ml_dtypes
import concourse.bass as bass
import concourse.mybir as mybir
from concourse.bass_utils import run_bass_kernel_spmd

F32 = mybir.dt.float32
F16 = mybir.dt.float16
BF16 = mybir.dt.bfloat16
U32 = mybir.dt.uint32
AF = mybir.ActivationFunctionType
ALU = mybir.AluOpType
AX = mybir.AxisListType

D = 2048
DFF = 8192
S = 4096
TOWN = 1024
MAGIC = 98304.0
MAGICH = MAGIC + 0.5
EPSQ = 1e-5
INV127 = 1.0 / 127.0

DEBUG = False


class Prog:
    ENG = ("sy", "gp", "pe", "ve", "se")
    DMAQ = ("sy", "gp", "se")

    def __init__(self, nc, es):
        self.nc = nc
        self.ops = {k: [] for k in self.ENG}
        self.cnt = {k: 0 for k in self.ENG}
        self.dcnt = {k: 0 for k in self.DMAQ}
        self.sems = {}
        for k in self.ENG:
            self.sems["E" + k] = es.enter_context(nc.semaphore("E" + k))
        for k in self.DMAQ:
            self.sems["D" + k] = es.enter_context(nc.semaphore("D" + k))

    def op(self, eng, emit, deps=(), dma=False):
        flat = []

        def _fl(d):
            if d is None:
                return
            if isinstance(d, tuple) and len(d) == 3 and isinstance(d[0], str):
                flat.append(d)
            elif isinstance(d, (list, tuple)):
                for x in d:
                    _fl(x)
            else:
                raise TypeError(f"bad dep {d!r}")
        _fl(deps)
        if dma:
            assert eng in self.DMAQ
            self.dcnt[eng] += 16
            ev = ("D" + eng, self.dcnt[eng], 16)
        else:
            self.cnt[eng] += 1
            ev = ("E" + eng, self.cnt[eng], 1)
        self.ops[eng].append((tuple(flat), emit, ev))
        return ev

    def emit_engine(self, key, eobj):
        hw = {}
        for deps, emit, ev in self.ops[key]:
            need = {}
            for sn, val, _ in deps:
                if sn == "E" + key and key == "pe":
                    continue
                if need.get(sn, 0) < val:
                    need[sn] = val
            for sn, val in need.items():
                if hw.get(sn, 0) < val:
                    eobj.wait_ge(self.sems[sn], val)
                    hw[sn] = val
            inst = emit(eobj)
            sn, val, inc = ev
            inst.then_inc(self.sems[sn], inc)


def build(debug=DEBUG):
    nc = bass.Bass("TRN2", target_bir_lowering=False)
    es = contextlib.ExitStack()

    dt_ = nc.dram_tensor
    x_batch = dt_("x_batch", [S, D], F32, kind="ExternalInput").ap()
    x_own = dt_("x_own", [TOWN, D], F32, kind="ExternalInput").ap()
    w_qkv = dt_("w_qkv", [3072, D], F32, kind="ExternalInput").ap()
    b_qkv = dt_("b_qkv", [3072], F32, kind="ExternalInput").ap()
    w_out = dt_("w_out", [D, D], F32, kind="ExternalInput").ap()
    b_out = dt_("b_out", [D], F32, kind="ExternalInput").ap()
    w_gate = dt_("w_gate", [DFF, D], F32, kind="ExternalInput").ap()
    w_up = dt_("w_up", [DFF, D], F32, kind="ExternalInput").ap()
    w_down = dt_("w_down", [D, DFF], F32, kind="ExternalInput").ap()
    out_own = dt_("out_own", [TOWN, D], F32, kind="ExternalOutput").ap()

    wqkv_bf = dt_("wqkv_bf", [3072, D], BF16).ap()
    wout_bf = dt_("wout_bf", [D, D], BF16).ap()
    xn_dram = dt_("xn_dram", [S, D], BF16).ap()
    xnown_dram = dt_("xnown_dram", [TOWN, D], BF16).ap()
    xq_dram = dt_("xq_dram", [TOWN, D], BF16).ap()
    t_dram = [dt_("tg_dram", [DFF, D], BF16).ap(),
              dt_("tu_dram", [DFF, D], BF16).ap(),
              dt_("td_dram", [D, DFF], BF16).ap()]
    h16_dram = dt_("h16_dram", [TOWN, DFF], F16).ap()
    hq_dram = dt_("hq_dram", [TOWN, DFF], BF16).ap()
    sg_dram = dt_("sg_dram", [D, TOWN], BF16).ap()
    sh_dram = dt_("sh_dram", [TOWN, 1], F32).ap()
    qhat_dram = dt_("qhat_dram", [2080, 1024], BF16).ap()

    a = nc.alloc_sbuf_tensor
    AR0 = a("AR0", [128, 16384], BF16)
    AR1 = a("AR1", [128, 16384], BF16)
    AR2 = a("AR2", [128, 16384], BF16)
    AR3 = a("AR3", [128, 16384], BF16)
    AR4 = a("AR4", [128, 16640], BF16)
    wt = a("wt", [128, 4096], F32)
    tmp_le = a("tmp_le", [128, 2048], F32)
    ttile = a("ttile", [128, 2048], BF16)
    ones = a("ones", [128, 128], F32)
    bk = a("bk", [128, 4], F32)
    bv = a("bv", [1, 512], F32)
    bq_eff = a("bq_eff", [128, 16], F32)
    bout_t = a("bout_t", [128, 16], F32)
    sx_all = a("sx_all", [128, 8], F32)
    sxg = a("sxg", [128, 8], F32)
    sxu = a("sxu", [128, 8], F32)
    shmax = a("shmax", [128, 8], F32)
    shsc = a("shsc", [128, 8], F32)
    thr = a("thr", [128, 8], F32)
    swsc = a("swsc", [1, 8], F32)
    p1acc = a("p1acc", [128, 4], F32)
    bnst = a("bnst", [128, 24], F32)
    mv = a("mv", [128, 2], F32)
    rstd = a("rstd", [128, 1], F32)
    o_un = a("o_un", [64, 1024], BF16)
    gu = a("gu", [128, 2048], F32)
    qb = a("qb", [128, 1024], F32)
    mrow = a("mrow", [1, 160], F32)

    PA = nc.alloc_psum_tensor("PA", [128, 2048], F32)
    PB = nc.alloc_psum_tensor("PB", [128, 2048], F32)

    def bankA(i):
        return PA[:, i * 512:(i + 1) * 512]

    def bankB(i):
        return PB[:, i * 512:(i + 1) * 512]

    def bank8(i):
        return bankA(i) if i < 4 else bankB(i - 4)

    AR0f = AR0[:].bitcast(F32)
    AR1f = AR1[:].bitcast(F32)
    AR2f = AR2[:].bitcast(F32)
    AR3f = AR3[:].bitcast(F32)
    AR3h = AR3[:].bitcast(F16)
    gub = gu[:].bitcast(BF16)

    den_sc = qb[0:1, 0:1024]
    bcast = gu[:, 0:1024]
    sh_row = gu[0:1, 1024:2048]
    p = Prog(nc, es)
    OP = p.op

    dbg = {}
    if debug:
        for nm, shp, dty in [("d_kt", [128, 16384], BF16), ("d_vt", [128, 16640], BF16),
                             ("d_qt", [2080, 1024], BF16), ("d_ot", [128, 16384], BF16),
                             ("d_s", [128, 16384], BF16), ("d_sw", [1, 8], F32),
                             ("d_sx", [128, 8], F32),
                             ("d_shsc", [128, 8], F32), ("d_xn", [S, D], BF16),
                             ("d_xq", [TOWN, D], BF16), ("d_hq", [TOWN, DFF], BF16),
                             ("d_sgd", [D, TOWN], BF16), ("d_tg", [DFF, D], BF16),
                             ("d_td", [D, DFF], BF16)]:
            dbg[nm] = dt_(nm, shp, dty, kind="ExternalOutput").ap()

    ev_ones = OP("ve", lambda e: e.memset(ones[:, :], 1.0))
    vt1 = AR4[:, :].rearrange("p (i g d) -> p i g d", g=8, d=65)
    ev_vt_ones = OP("ve", lambda e: e.memset(vt1[:, :, :, 64:65], 1.0))

    ev_b = []
    for c in range(4):
        ev_b.append(OP("sy", (lambda e, c=c: e.dma_start(
            out=bk[:, c:c + 1],
            in_=b_qkv[2048 + c * 128:2048 + (c + 1) * 128].rearrange("(a b) -> a b", b=1))), dma=True))
    ev_b.append(OP("sy", lambda e: e.dma_start(
        out=bv[:, :], in_=b_qkv[2560:3072].rearrange("(a b) -> a b", a=1)), dma=True))
    ev_bq_raw = []
    for c in range(16):
        ev_bq_raw.append(OP("sy", (lambda e, c=c: e.dma_start(
            out=bq_eff[:, c:c + 1],
            in_=b_qkv[c * 128:(c + 1) * 128].rearrange("(a b) -> a b", b=1))), dma=True))
        ev_b.append(OP("sy", (lambda e, c=c: e.dma_start(
            out=bout_t[:, c:c + 1],
            in_=b_out[c * 128:(c + 1) * 128].rearrange("(a b) -> a b", b=1))), dma=True))
    ev_bq = OP("ve", lambda e: e.tensor_scalar(bq_eff[:, :], bq_eff[:, :], 0.125, None, ALU.mult),
               deps=ev_bq_raw)

    ev_cast_qkv = []
    for i in range(12):
        ev_cast_qkv.append(OP("gp", (lambda e, i=i: e.dma_start(
            out=wqkv_bf[i * 256:(i + 1) * 256, :].rearrange("a (b c) -> a b c", c=512),
            in_=w_qkv[i * 256:(i + 1) * 256, :].rearrange("a (b c) -> a b c", c=512))), dma=True))
    ev_cast_out = []
    for i in range(8):
        ev_cast_out.append(OP("gp", (lambda e, i=i: e.dma_start(
            out=wout_bf[i * 256:(i + 1) * 256, :].rearrange("a (b c) -> a b c", c=512),
            in_=w_out[i * 256:(i + 1) * 256, :].rearrange("a (b c) -> a b c", c=512))), dma=True))

    TERN = ([(0, rb, 0) for rb in range(DFF // 128)] +
            [(1, rb, 0) for rb in range(DFF // 128)] +
            [(2, rb, cb) for rb in range(D // 128) for cb in range(4)])

    def tsrc(idx):
        w, rb, cb = TERN[idx]
        wsrc = (w_gate, w_up, w_down)[w]
        if w < 2:
            return wsrc[rb * 128:(rb + 1) * 128, :]
        return wsrc[rb * 128:(rb + 1) * 128, cb * 2048:(cb + 1) * 2048]

    def tdst(idx):
        w, rb, cb = TERN[idx]
        dd = t_dram[w]
        if w < 2:
            return dd[rb * 128:(rb + 1) * 128, :]
        return dd[rb * 128:(rb + 1) * 128, cb * 2048:(cb + 1) * 2048]

    ev_p1z = OP("ve", lambda e: e.memset(p1acc[:, :], 0.0))
    p1 = {"i": 0, "lastacc": [ev_p1z, ev_p1z, ev_p1z], "wfree": [None, None]}

    def emit_p1_tiles(n):
        for _ in range(n):
            i = p1["i"]
            if i >= len(TERN):
                return
            p1["i"] = i + 1
            w = TERN[i][0]
            buf = i % 2
            ld = OP("sy", (lambda e, i=i, buf=buf: e.dma_start(
                out=wt[:, buf * 2048:(buf + 1) * 2048], in_=tsrc(i))),
                deps=(p1["wfree"][buf],), dma=True)
            acc = OP("ve", (lambda e, buf=buf: e.scalar_tensor_tensor(
                tmp_le[:, :], wt[:, buf * 2048:(buf + 1) * 2048], -1.0,
                wt[:, buf * 2048:(buf + 1) * 2048], ALU.mult, ALU.max,
                accum_out=qb[:, buf:buf + 1])),
                deps=(ld,))
            acc2 = OP("ve", (lambda e, w=w, buf=buf: e.tensor_add(
                p1acc[:, w:w + 1], p1acc[:, w:w + 1], qb[:, buf:buf + 1])),
                deps=(acc, p1["lastacc"][w]))
            p1["lastacc"][w] = acc2
            p1["wfree"][buf] = acc

    ln_free = [ev_p1z, ev_p1z]
    xn_war = None
    qz_war = None
    rstd_war = None
    ev_xn_done = []
    ev_xq_done = []
    ev_sx = []
    for i in range(40):
        buf = i % 2
        src = (x_batch[i * 128:(i + 1) * 128, :] if i < 32 else
               x_own[(i - 32) * 128:(i - 31) * 128, :])
        ld = OP("sy", (lambda e, src=src, buf=buf: e.dma_start(
            out=AR0f[:, buf * 2048:(buf + 1) * 2048], in_=src)),
            deps=(ln_free[buf],), dma=True)
        st = None
        for k in range(4):
            st = OP("ve", (lambda e, buf=buf, k=k: e.bn_stats(
                bnst[:, k * 6:(k + 1) * 6],
                AR0f[:, buf * 2048 + k * 512: buf * 2048 + (k + 1) * 512])),
                deps=(ld,))
        ag = OP("ve", (lambda e: e.bn_aggr(mv[:, :], bnst[:, :])), deps=(st,))
        t1 = OP("ve", (lambda e: e.tensor_scalar(rstd[:, :], mv[:, 1:2], 1e-5, None, ALU.add)),
                deps=(ag, rstd_war))
        t2 = OP("se", (lambda e: e.activation(rstd[:, :], rstd[:, :], AF.Sqrt)), deps=(t1,))
        t3 = OP("ve", (lambda e: e.reciprocal(rstd[:, :], rstd[:, :])), deps=(t2,))
        rstd_war = t3
        nm = OP("ve", (lambda e, buf=buf: e.tensor_scalar(
            AR0f[:, 4096:6144], AR0f[:, buf * 2048:(buf + 1) * 2048],
            mv[:, 0:1], rstd[:, 0:1], ALU.subtract, ALU.mult)),
            deps=(t3, xn_war))
        ln_free[buf] = nm
        dst = (xn_dram[i * 128:(i + 1) * 128, :] if i < 32 else
               xnown_dram[(i - 32) * 128:(i - 31) * 128, :])
        cd = OP("gp", (lambda e, dst=dst: e.dma_start(
            out=dst.rearrange("a (b c) -> a b c", c=512),
            in_=AR0f[:, 4096:6144].rearrange("a (b c) -> a b c", c=512))),
            deps=(nm,), dma=True)
        xn_war = cd
        ev_xn_done.append(cd)
        if i >= 32:
            j = i - 32
            mx = OP("ve", (lambda e: e.reduce_max(rstd[:, :], AR0f[:, 4096:6144],
                                                  axis=AX.X, apply_absolute_value=True)),
                    deps=(nm, t3))
            rstd_war = mx
            mx2 = OP("ve", (lambda e: e.tensor_scalar(rstd[:, :], rstd[:, :], EPSQ, INV127,
                                                      ALU.max, ALU.mult)), deps=(mx,))
            sx = OP("ve", (lambda e, j=j: e.tensor_copy(sx_all[:, j:j + 1], rstd[:, :])),
                    deps=(mx2,))
            ev_sx.append(sx)
            rr = OP("ve", (lambda e: e.reciprocal(rstd[:, :], rstd[:, :])), deps=(sx,))
            rstd_war = rr
            q1 = OP("ve", (lambda e: e.tensor_scalar(
                AR0f[:, 6144:8192], AR0f[:, 4096:6144], rstd[:, 0:1], MAGICH,
                ALU.mult, ALU.add)), deps=(rr, cd))
            xn_war = q1
            q2 = OP("ve", (lambda e: e.tensor_scalar(
                AR0f[:, 6144:8192].bitcast(U32), AR0f[:, 6144:8192].bitcast(U32),
                0xFFFFFF80, None, ALU.bitwise_and)), deps=(q1,))
            q3 = OP("ve", (lambda e: e.tensor_scalar(
                gub[:, 0:2048], AR0f[:, 6144:8192], -MAGIC, None, ALU.add)),
                deps=(q2, qz_war))
            qd = OP("gp", (lambda e, j=j: e.dma_start(
                out=xq_dram[j * 128:(j + 1) * 128, :], in_=gub[:, 0:2048])),
                deps=(q3,), dma=True)
            qz_war = qd
            ev_xq_done.append(qd)
        emit_p1_tiles(2)

    ev_wkvT = []
    for c in range(16):
        ev_wkvT.append(OP("se", (lambda e, c=c: e.dma_start_transpose(
            AR2[:, c * 1024:(c + 1) * 1024],
            wqkv_bf[2048:3072, c * 128:(c + 1) * 128])),
            deps=ev_cast_qkv, dma=True))

    xnT_war = [None, None]
    kbank_free = [None] * 4
    vbank_free = [None] * 4
    ev_K_ev = []
    ev_V_ev = []
    for tt in range(8):
        half = tt % 2
        base = half * 8192
        ev_xt = []
        for c in range(16):
            ev_xt.append(OP("se", (lambda e, c=c, tt=tt, base=base: e.dma_start_transpose(
                AR1[:, base + c * 512: base + (c + 1) * 512],
                xn_dram[tt * 512:(tt + 1) * 512, c * 128:(c + 1) * 128])),
                deps=(ev_xn_done[tt * 4 + 3], xnT_war[half]), dma=True))

        for m in range(4):
            def k_mms(e, m=m, base=base):
                r = None
                for k in range(16):
                    r = e.matmul(bankA(m),
                                 AR2[:, k * 1024 + m * 128: k * 1024 + (m + 1) * 128],
                                 AR1[:, base + k * 512: base + (k + 1) * 512],
                                 start=(k == 0), stop=(k == 15))
                return r
            mm = OP("pe", k_mms, deps=(ev_wkvT, ev_xt, kbank_free[m]))
            evc = OP("se", (lambda e, m=m, tt=tt: e.activation(
                AR3[:, m * 4096 + tt * 512: m * 4096 + (tt + 1) * 512],
                bankA(m), AF.Identity, bias=bk[:, m:m + 1], scale=1.0)),
                deps=(mm, ev_b))
            kbank_free[m] = evc
            ev_K_ev.append(evc)

        for m in range(4):
            def v_mms(e, m=m, base=base):
                r = e.matmul(bankB(m), ones[0:1, 0:128], bv[0:1, :], start=True, stop=False)
                for k in range(16):
                    r = e.matmul(bankB(m),
                                 AR1[:, base + k * 512 + m * 128: base + k * 512 + (m + 1) * 128],
                                 AR2[:, k * 1024 + 512: (k + 1) * 1024],
                                 start=False, stop=(k == 15))
                return r
            mm = OP("pe", v_mms, deps=(ev_wkvT, ev_xt, vbank_free[m], ev_ones, ev_b))
            vchunk = tt * 4 + m
            vdst = AR4[:, vchunk * 520: vchunk * 520 + 520].rearrange(
                "p (g d) -> p g d", g=8)[:, :, 0:64]
            vsrc = PB[:, m * 512:(m + 1) * 512].rearrange("p (g d) -> p g d", g=8)
            evc = OP("ve", (lambda e, vdst=vdst, vsrc=vsrc: e.tensor_copy(vdst, vsrc)),
                     deps=(mm, ev_vt_ones))
            vbank_free[m] = evc
            ev_V_ev.append(evc)
        xnT_war[half] = ev_V_ev[-1]
        emit_p1_tiles(14)
    emit_p1_tiles(len(TERN))

    p1_sum = OP("pe", (lambda e: e.matmul(PB[0:1, 0:3], ones[:, 0:1], p1acc[:, 0:3],
                                          start=True, stop=True)),
                deps=(p1["lastacc"], vbank_free[0], ev_ones))
    p1_sc = OP("ve", (lambda e: e.tensor_scalar(swsc[0:1, 0:3], PB[0:1, 0:3],
                                                1.0 / (DFF * D), EPSQ, ALU.mult, ALU.add)),
               deps=(p1_sum,))
    thr_mm = OP("pe", (lambda e: e.matmul(PB[:, 1024:1027], ones[0:1, :], swsc[0:1, 0:3],
                                          start=True, stop=True)),
                deps=(p1_sc, vbank_free[2], ev_ones))
    thr_p = OP("ve", (lambda e: e.tensor_scalar(thr[:, 0:3], PB[:, 1024:1027], 0.5, None, ALU.mult)),
               deps=(thr_mm,))
    thr_n = OP("ve", (lambda e: e.tensor_scalar(thr[:, 4:7], PB[:, 1024:1027], -0.5, None, ALU.mult)),
               deps=(thr_mm,))
    vbank_free[0] = p1_sc
    vbank_free[2] = thr_n

    tern = {"i": 0, "free": [None, None], "tfree": [None, None]}

    def emit_tern_tiles(n):
        for _ in range(n):
            i = tern["i"]
            if i >= len(TERN):
                return
            tern["i"] = i + 1
            w = TERN[i][0]
            buf = i % 2
            ld = OP("sy", (lambda e, i=i, buf=buf: e.dma_start(
                out=wt[:, buf * 2048:(buf + 1) * 2048], in_=tsrc(i))),
                deps=(tern["free"][buf], thr_p, thr_n, p1["wfree"][buf]), dma=True)
            c1 = OP("ve", (lambda e, buf=buf, w=w: e.tensor_scalar(
                tmp_le[:, :], wt[:, buf * 2048:(buf + 1) * 2048],
                thr[:, 4 + w:5 + w], None, ALU.is_le)), deps=(ld,))
            c2 = OP("ve", (lambda e, buf=buf, w=w: e.scalar_tensor_tensor(
                ttile[:, 0:2048],
                wt[:, buf * 2048:(buf + 1) * 2048], thr[:, w:w + 1],
                tmp_le[:, :], ALU.is_ge, ALU.subtract)),
                deps=(c1, tern["tfree"][0], tern["tfree"][1]))
            tern["free"][buf] = c2
            stq = OP("gp", (lambda e, i=i: e.dma_start(
                out=tdst(i), in_=ttile[:, 0:2048])),
                deps=(c2,), dma=True)
            tern["tfree"][0] = stq

    ev_xnownT = []
    for c in range(16):
        ev_xnownT.append(OP("se", (lambda e, c=c: e.dma_start_transpose(
            AR2[:, c * 1024:(c + 1) * 1024],
            xnown_dram[:, c * 128:(c + 1) * 128])),
            deps=(ev_xn_done[39], ev_K_ev, ev_V_ev), dma=True))

    ev_QT = []
    b8_free = kbank_free + vbank_free
    wq_war = [xnT_war[0], xnT_war[1]]
    for qg in range(4):
        half = qg % 2
        base = half * 8192
        ev_wq = []
        for k in range(16):
            ev_wq.append(OP("se", (lambda e, qg=qg, k=k, base=base: e.dma_start_transpose(
                AR1[:, base + k * 512: base + (k + 1) * 512],
                wqkv_bf[qg * 512:(qg + 1) * 512, k * 128:(k + 1) * 128])),
                deps=(ev_cast_qkv, wq_war[half]), dma=True))

        def q_mms(e, base=base):
            r = None
            for qc in range(4):
                for nn in range(2):
                    for k in range(16):
                        r = e.matmul(bank8(qc * 2 + nn),
                                     AR1[:, base + k * 512 + qc * 128: base + k * 512 + (qc + 1) * 128],
                                     AR2[:, k * 1024 + nn * 512: k * 1024 + (nn + 1) * 512],
                                     start=(k == 0), stop=(k == 15))
            return r
        mm = OP("pe", q_mms, deps=(ev_wq, ev_xnownT, b8_free))
        evs = []
        for qc in range(4):
            for nn in range(2):
                c = qg * 4 + qc
                evs.append(OP("se", (lambda e, qc=qc, nn=nn, c=c: e.activation(
                    AR0[:, c * 1024 + nn * 512: c * 1024 + (nn + 1) * 512],
                    bank8(qc * 2 + nn), AF.Identity, bias=bq_eff[:, c:c + 1], scale=0.125)),
                    deps=(mm, ev_bq, ev_xq_done[-1] if qg == 0 else None)))
        b8_free = evs
        wq_war[half] = evs[-1]
        ev_QT.extend(evs)

    ev_bparts = []
    sq_war = thr_p
    red_war = None
    pa01_war = [b8_free, b8_free]
    last_sqr = None
    for g in range(8):
        j, p2 = g // 2, g % 2
        sq = OP("se", (lambda e, j=j, p2=p2: e.activation(
            AR1f[0:64, 0:4096], AR3[p2 * 64:(p2 + 1) * 64, j * 4096:(j + 1) * 4096],
            AF.Square)),
            deps=(ev_K_ev, sq_war, wq_war[0], wq_war[1]))
        parts = []
        for cc in range(8):
            s1 = OP("pe", (lambda e, cc=cc: e.matmul(
                PA[0:1, (cc % 2) * 512:(cc % 2 + 1) * 512], ones[0:64, 0:1],
                AR1f[0:64, cc * 512:(cc + 1) * 512], start=True, stop=True)),
                deps=(sq, pa01_war[cc % 2], ev_ones))
            r1 = OP("ve", (lambda e, g=g, cc=cc: e.reduce_max(
                mrow[0:1, 80 + cc:81 + cc], PA[0:1, (cc % 2) * 512:(cc % 2 + 1) * 512],
                axis=AX.X)), deps=(s1, red_war))
            pa01_war[cc % 2] = r1
            red_war = r1
            parts.append(r1)
        km = OP("ve", (lambda e, g=g: e.reduce_max(
            mrow[0:1, 64 + g:65 + g], mrow[0:1, 80:88], axis=AX.X)), deps=(parts,))
        red_war = km
        sq_war = km
        ev_bparts.append(km)
    kmx = OP("se", (lambda e: e.activation(mrow[0:1, 72:80], mrow[0:1, 64:72], AF.Sqrt)),
             deps=(ev_bparts,))
    ev_bparts.append(kmx)

    qhat_exports = []
    mq_war = None
    for c in range(16):
        sq2 = OP("se", (lambda e, c=c: e.activation(
            qb[:, :], AR0[:, c * 1024:(c + 1) * 1024], AF.Square)),
            deps=(ev_QT[c * 2], ev_QT[c * 2 + 1], kmx, sq_war))
        sq_war = sq2
        for p2 in range(2):
            h = c * 2 + p2
            g = 2 * (c // 4) + p2
            s1 = OP("pe", (lambda e, p2=p2: e.matmul(
                PA[0:1, 0:512], ones[p2 * 64:(p2 + 1) * 64, 0:1],
                qb[p2 * 64:(p2 + 1) * 64, 0:512], start=True, stop=True)),
                deps=(sq2, pa01_war[0], ev_ones))
            s2 = OP("pe", (lambda e, p2=p2: e.matmul(
                PA[0:1, 512:1024], ones[p2 * 64:(p2 + 1) * 64, 0:1],
                qb[p2 * 64:(p2 + 1) * 64, 512:1024], start=True, stop=True)),
                deps=(sq2, pa01_war[1], ev_ones))
            sqr = OP("se", (lambda e: e.activation(gu[0:1, 0:1024], PA[0:1, 0:1024],
                                                   AF.Sqrt)),
                     deps=(s1, s2, mq_war, ev_xq_done))
            pa01_war = [sqr, sqr]
            mqr = OP("ve", (lambda e, g=g: e.tensor_scalar(
                gub[0:1, 2048:3072], gu[0:1, 0:1024],
                mrow[0:1, 72 + g:73 + g], -1.0, ALU.mult, ALU.mult)),
                deps=(sqr, kmx))
            mq_war = mqr
            qd1 = OP("gp", (lambda e, h=h: e.dma_start(
                out=qhat_dram[h * 65 + 64:h * 65 + 65, :], in_=gub[0:1, 2048:3072])),
                deps=(mqr,), dma=True)
            qd2 = OP("gp", (lambda e, h=h, c=c, p2=p2: e.dma_start(
                out=qhat_dram[h * 65:h * 65 + 64, :],
                in_=AR0[p2 * 64:(p2 + 1) * 64, c * 1024:(c + 1) * 1024])),
                deps=(ev_QT[c * 2], ev_QT[c * 2 + 1]), dma=True)
            qhat_exports.extend([qd1, qd2])
            mq_war = qd1

    ev_conv = []
    for j in range(4):
        cpo = OP("ve", (lambda e, j=j: e.tensor_copy(
            AR0[0:64, j * 4096:(j + 1) * 4096],
            AR3[64:128, j * 4096:(j + 1) * 4096])),
            deps=(qhat_exports, ev_bparts, red_war))
        m1 = OP("ve", (lambda e, j=j: e.memset(
            AR3[64:65, j * 4096:(j + 1) * 4096], 1.0)), deps=(cpo,))
        m2 = OP("ve", (lambda e, j=j: e.memset(
            AR0[64:65, j * 4096:(j + 1) * 4096], 1.0)), deps=(cpo,))
        ev_conv.extend([cpo, m1, m2])

    def khat(g):
        ar = AR3 if g % 2 == 0 else AR0
        return ar[:, (g // 2) * 4096:(g // 2 + 1) * 4096]

    ev_attn_done = []
    pa_war = pa01_war[0]
    av_war = [thr_n, thr_n]
    pb2_war = thr_n
    pslot_war = [wq_war[0], wq_war[1]]
    qbuf_war = [red_war, red_war]
    qld_prev = {}
    for h in range(32):
        c, p2 = h // 2, h % 2
        j, r = c // 4, c % 4
        g = 2 * j + p2
        qs = h % 2
        qld = OP("sy", (lambda e, h=h, qs=qs: e.dma_start(
            out=AR1[0:65, 4096 + qs * 1024: 4096 + qs * 1024 + 1024],
            in_=qhat_dram[h * 65:(h + 1) * 65, :])),
            deps=(qhat_exports[2 * h], qhat_exports[2 * h + 1], qbuf_war[qs]), dma=True)
        for qt in range(2):
            bank = (h * 2 + qt) % 2
            av_last = None
            for sc in range(8):
                slot = sc % 2

                def sc_mms(e, g=g, qs=qs, qt=qt, sc=sc):
                    rr = None
                    for kk in range(4):
                        kchunk = sc * 4 + kk
                        rr = e.matmul(
                            PA[:, kk * 512:(kk + 1) * 512],
                            khat(g)[0:65, kchunk * 128:(kchunk + 1) * 128],
                            AR1[0:65, 4096 + qs * 1024 + qt * 512:
                                4096 + qs * 1024 + (qt + 1) * 512],
                            start=True, stop=True)
                    return rr
                mm = OP("pe", sc_mms, deps=(pa_war, qld, ev_conv))
                ex = OP("se", (lambda e, slot=slot: e.activation(
                    AR1[:, slot * 2048:(slot + 1) * 2048], PA[:, :], AF.Exp)),
                    deps=(mm, pslot_war[slot]))
                pa_war = ex

                def av_mms(e, g=g, sc=sc, slot=slot, bank=bank):
                    rr = None
                    for kk in range(4):
                        kchunk = sc * 4 + kk
                        vbase = kchunk * 520
                        rr = e.matmul(
                            PB[0:65, bank * 512:(bank + 1) * 512],
                            AR4[:, vbase + g * 65: vbase + (g + 1) * 65],
                            AR1[:, slot * 2048 + kk * 512: slot * 2048 + (kk + 1) * 512],
                            start=(sc == 0 and kk == 0), stop=(sc == 7 and kk == 3))
                    return rr
                av = OP("pe", av_mms,
                        deps=(ex,) + ((av_war[bank],) if sc == 0 else ()))
                pslot_war[slot] = av
                av_last = av
            qbuf_war[qs] = av_last
            cp = OP("ve", (lambda e, bank=bank: e.tensor_copy(
                o_un[:, bank * 512:(bank + 1) * 512],
                PB[0:64, bank * 512:(bank + 1) * 512])),
                deps=(av_last,))
            rc = OP("ve", (lambda e, bank=bank: e.reciprocal(
                den_sc[0:1, bank * 512:(bank + 1) * 512],
                PB[64:65, bank * 512:(bank + 1) * 512])), deps=(av_last, cp))
            bc = OP("pe", (lambda e, bank=bank: e.matmul(
                PB[0:64, 1024:1536], ones[0:1, 0:64],
                den_sc[0:1, bank * 512:(bank + 1) * 512], start=True, stop=True)),
                deps=(rc, pb2_war))
            ot = OP("ve", (lambda e, p2=p2, c=c, qt=qt, bank=bank: e.tensor_mul(
                AR2[p2 * 64:(p2 + 1) * 64, c * 1024 + qt * 512: c * 1024 + (qt + 1) * 512],
                o_un[:, bank * 512:(bank + 1) * 512],
                PB[0:64, 1024:1536])), deps=(bc, cp))
            pb2_war = ot
            av_war[bank] = ot
            ev_attn_done.append(ot)
            emit_tern_tiles(3)
    emit_tern_tiles(len(TERN))

    if debug:
        dbg_evs = []
        dbg_evs.append(OP("gp", (lambda e: e.dma_start(out=dbg["d_kt"], in_=AR3[:, :])),
                          deps=(ev_attn_done,), dma=True))
        dbg_evs.append(OP("gp", (lambda e: e.dma_start(out=dbg["d_vt"], in_=AR4[:, :])),
                          deps=(ev_attn_done,), dma=True))
        dbg_evs.append(OP("gp", (lambda e: e.dma_start(
            out=dbg["d_qt"].rearrange("a (b c) -> a b c", c=512),
            in_=qhat_dram.rearrange("a (b c) -> a b c", c=512))),
            deps=(qhat_exports,), dma=True))
        dbg_evs.append(OP("gp", (lambda e: e.dma_start(out=dbg["d_ot"], in_=AR2[:, :])),
                          deps=(ev_attn_done,), dma=True))
        dbg_evs.append(OP("gp", (lambda e: e.dma_start(out=dbg["d_sx"], in_=sx_all[:, :])),
                          deps=(ev_sx,), dma=True))
        dbg_evs.append(OP("gp", (lambda e: e.dma_start(out=dbg["d_sw"], in_=swsc[0:1, :])),
                          deps=(p1_sc, thr_n), dma=True))
    else:
        dbg_evs = []


    ev_s = []
    wo_war = [pslot_war[0], pslot_war[1]]
    b8_free = [pa_war, pa_war, pa_war, pa_war,
               av_war[0], av_war[1], pb2_war, thr_n]
    s_war = av_war + [pb2_war]
    for og in range(4):
        half = og % 2
        base = half * 8192
        ev_wo = []
        for k in range(16):
            ev_wo.append(OP("se", (lambda e, og=og, k=k, base=base: e.dma_start_transpose(
                AR1[:, base + k * 512: base + (k + 1) * 512],
                wout_bf[og * 512:(og + 1) * 512, k * 128:(k + 1) * 128])),
                deps=(ev_cast_out, wo_war[half]), dma=True))

        def o_mms(e, base=base):
            rr = None
            for oc in range(4):
                for nn in range(2):
                    for k in range(16):
                        rr = e.matmul(bank8(oc * 2 + nn),
                                      AR1[:, base + k * 512 + oc * 128: base + k * 512 + (oc + 1) * 128],
                                      AR2[:, k * 1024 + nn * 512: k * 1024 + (nn + 1) * 512],
                                      start=(k == 0), stop=(k == 15))
            return rr
        mm = OP("pe", o_mms, deps=(ev_wo, ev_attn_done, b8_free, dbg_evs))
        evs = []
        for oc in range(4):
            for nn in range(2):
                c = og * 4 + oc
                evs.append(OP("se", (lambda e, oc=oc, nn=nn, c=c: e.activation(
                    AR4[:, c * 1024 + nn * 512: c * 1024 + (nn + 1) * 512],
                    bank8(oc * 2 + nn), AF.Identity, bias=bout_t[:, c:c + 1], scale=1.0)),
                    deps=(mm, ev_b) + ((tuple(s_war),) if og == 0 else ())))
        b8_free = evs
        wo_war[half] = evs[-1]
        ev_s.extend(evs)

    ev_xqT = []
    for c in range(16):
        ev_xqT.append(OP("se", (lambda e, c=c: e.dma_start_transpose(
            AR0[:, c * 1024:(c + 1) * 1024],
            xq_dram[:, c * 128:(c + 1) * 128])),
            deps=(ev_xq_done, ev_attn_done, ev_bparts), dma=True))
    swg_b = OP("ve", (lambda e: e.tensor_scalar(qb[:, 0:3], thr[:, 0:3], 2.0, None, ALU.mult)),
               deps=(thr_p, red_war, ev_bparts, ev_attn_done))
    ev_sxg = OP("ve", (lambda e: e.tensor_scalar(sxg[:, :], sx_all[:, :], qb[:, 0:1], None, ALU.mult)),
                deps=(swg_b, ev_sx))
    ev_sxu = OP("ve", (lambda e: e.tensor_scalar(sxu[:, :], sx_all[:, :], qb[:, 1:2], None, ALU.mult)),
                deps=(swg_b, ev_sx))
    ev_shz = OP("ve", (lambda e: e.memset(shmax[:, :], 0.0)))

    ev_h16 = []
    g_war = [wo_war[0], wo_war[1]]
    u_war = [ev_s[-1], ev_s[-1]]
    gu_war = [ev_shz] * 4
    h_war = [None] * 4
    pg_free = ev_s[-2]
    pu_free = ev_s[-1]
    shm_dep = {tc: ev_shz for tc in range(8)}
    last_gu_mm = None
    qb4_war = ev_sxu
    for seg in range(16):
        half = seg % 2
        base = half * 8192
        ev_tg = []
        ev_tu = []
        for k in range(16):
            ev_tg.append(OP("se", (lambda e, seg=seg, k=k, base=base: e.dma_start_transpose(
                AR1[:, base + k * 512: base + (k + 1) * 512],
                t_dram[0][seg * 512:(seg + 1) * 512, k * 128:(k + 1) * 128])),
                deps=(tern["tfree"], g_war[half]), dma=True))
            ev_tu.append(OP("se", (lambda e, seg=seg, k=k, base=base: e.dma_start_transpose(
                AR2[:, base + k * 512: base + (k + 1) * 512],
                t_dram[1][seg * 512:(seg + 1) * 512, k * 128:(k + 1) * 128])),
                deps=(tern["tfree"], u_war[half]), dma=True))
        for tc in range(8):
            gslot = tc % 2
            uslot = 2 + tc % 2
            hslot = tc % 4

            def g_mm(e, tc=tc, base=base):
                rr = None
                for k in range(16):
                    rr = e.matmul(bankA(tc % 4),
                                  AR0[:, k * 1024 + tc * 128: k * 1024 + (tc + 1) * 128],
                                  AR1[:, base + k * 512: base + (k + 1) * 512],
                                  start=(k == 0), stop=(k == 15))
                return rr

            def u_mm(e, tc=tc, base=base):
                rr = None
                for k in range(16):
                    rr = e.matmul(bankB(tc % 4),
                                  AR0[:, k * 1024 + tc * 128: k * 1024 + (tc + 1) * 128],
                                  AR2[:, base + k * 512: base + (k + 1) * 512],
                                  start=(k == 0), stop=(k == 15))
                return rr
            mmg = OP("pe", g_mm, deps=(ev_tg, ev_xqT, pg_free))
            mmu = OP("pe", u_mm, deps=(ev_tu, ev_xqT, pu_free))
            ge = OP("se", (lambda e, tc=tc, gslot=gslot: e.activation(
                gu[:, gslot * 512:(gslot + 1) * 512], bankA(tc % 4), AF.Sigmoid,
                bias=0.0, scale=sxg[:, tc:tc + 1])),
                deps=(mmg, ev_sxg, gu_war[gslot]))
            ue = OP("ve", (lambda e, tc=tc, uslot=uslot: e.tensor_scalar(
                gu[:, uslot * 512:(uslot + 1) * 512], bankB(tc % 4),
                sxu[:, tc:tc + 1], None, ALU.mult)),
                deps=(mmu, ev_sxu, gu_war[uslot]))
            pg_free = ge
            pu_free = ue
            hm = OP("ve", (lambda e, gslot=gslot, uslot=uslot, hslot=hslot: e.tensor_mul(
                AR3f[:, hslot * 512:(hslot + 1) * 512],
                gu[:, gslot * 512:(gslot + 1) * 512],
                gu[:, uslot * 512:(uslot + 1) * 512])),
                deps=(ge, ue, h_war[hslot]))
            gu_war[gslot] = hm
            gu_war[uslot] = hm
            mxop = OP("ve", (lambda e, hslot=hslot: e.reduce_max(
                qb[:, 4:5], AR3f[:, hslot * 512:(hslot + 1) * 512], axis=AX.X,
                apply_absolute_value=True)), deps=(hm, qb4_war))
            mxac = OP("ve", (lambda e, tc=tc: e.tensor_max(
                shmax[:, tc:tc + 1], shmax[:, tc:tc + 1], qb[:, 4:5])),
                deps=(mxop, shm_dep[tc]))
            shm_dep[tc] = mxac
            qb4_war = mxac
            hd = OP("gp", (lambda e, seg=seg, tc=tc, hslot=hslot: e.dma_start(
                out=h16_dram[tc * 128:(tc + 1) * 128, seg * 512:(seg + 1) * 512],
                in_=AR3f[:, hslot * 512:(hslot + 1) * 512])),
                deps=(hm,), dma=True)
            h_war[hslot] = hd
            ev_h16.append(hd)
            last_gu_mm = mmu
        g_war[half] = last_gu_mm
        u_war[half] = last_gu_mm

    ev_hq = []
    ev_shw = []
    prev_hq = None
    prev_q3 = None
    rstd_war2 = rstd_war
    for tc in range(8):
        ld = OP("sy", (lambda e, tc=tc: e.dma_start(
            out=AR3h[:, 0:8192].rearrange("a (b c) -> a b c", c=512),
            in_=h16_dram[tc * 128:(tc + 1) * 128, :].rearrange("a (b c) -> a b c", c=512))),
            deps=((ev_h16, last_gu_mm) if tc == 0 else (prev_q3,)), dma=True)
        s1 = OP("ve", (lambda e, tc=tc: e.tensor_scalar(
            shsc[:, tc:tc + 1], shmax[:, tc:tc + 1], EPSQ, INV127, ALU.max, ALU.mult)),
            deps=(shm_dep[tc],))
        s2 = OP("ve", (lambda e, tc=tc: e.reciprocal(rstd[:, :], shsc[:, tc:tc + 1])),
                deps=(s1, prev_hq, rstd_war2))
        rstd_war2 = None
        q1 = OP("ve", (lambda e: e.tensor_scalar(
            AR2f[:, 0:8192], AR3h[:, 0:8192], rstd[:, 0:1], MAGICH, ALU.mult, ALU.add)),
            deps=(ld, s2) + ((tuple(ev_s), last_gu_mm) if tc == 0 else ()))
        q2 = OP("ve", (lambda e: e.tensor_scalar(
            AR2f[:, 0:8192].bitcast(U32), AR2f[:, 0:8192].bitcast(U32),
            0xFFFFFF80, None, ALU.bitwise_and)), deps=(q1,))
        q3 = OP("ve", (lambda e: e.tensor_scalar(
            AR0[:, 0:8192], AR2f[:, 0:8192], -MAGIC, None, ALU.add)),
            deps=(q2,) + ((last_gu_mm,) if tc == 0 else ()))
        prev_q3 = q3
        hw = OP("gp", (lambda e, tc=tc: e.dma_start(
            out=hq_dram[tc * 128:(tc + 1) * 128, :].rearrange("a (b c) -> a b c", c=512),
            in_=AR0[:, 0:8192].rearrange("a (b c) -> a b c", c=512))),
            deps=(q3,), dma=True)
        prev_hq = hw
        ev_hq.append(hw)
        ev_shw.append(OP("gp", (lambda e, tc=tc: e.dma_start(
            out=sh_dram[tc * 128:(tc + 1) * 128, :], in_=shsc[:, tc:tc + 1])),
            deps=(s1,), dma=True))

    if debug:
        dbg_evs2 = [OP("gp", (lambda e: e.dma_start(out=dbg["d_s"], in_=AR4[:, 0:16384])),
                    deps=(ev_s,), dma=True),
                    OP("gp", (lambda e: e.dma_start(out=dbg["d_shsc"], in_=shsc[:, :])),
                    deps=(ev_shw,), dma=True)]
    else:
        dbg_evs2 = []

    ev_fin_sg = []
    sh_ld = OP("sy", (lambda e: e.dma_start(
        out=sh_row[0:1, :], in_=sh_dram[:, 0:1].rearrange("a b -> b a"))),
        deps=(ev_shw,), dma=True)

    def hqT_ap(k2):
        if k2 < 32:
            return AR2[:, k2 * 512:(k2 + 1) * 512]
        return AR3[:, (k2 - 32) * 512:(k2 - 32 + 1) * 512]

    td_war = [prev_q3, prev_q3]
    td_war = [ev_s[-1], ev_s[-1]]
    td_war = [last_gu_mm, last_gu_mm]
    sg_war = [last_gu_mm, last_gu_mm]
    bc_war = None
    dsc_war = tuple(ev_attn_done)
    hq_prev_half_mm = None
    pd_free = [prev_q3] * 8
    for th in range(2):
        ev_hqT = []
        for k2 in range(64):
            ev_hqT.append(OP("se", (lambda e, k2=k2, th=th: e.dma_start_transpose(
                hqT_ap(k2), hq_dram[th * 512:(th + 1) * 512, k2 * 128:(k2 + 1) * 128])),
                deps=(ev_hq,) + ((hq_prev_half_mm,) if th > 0 else (prev_q3, prev_hq)), dma=True))
        bs1 = OP("ve", (lambda e, th=th: e.tensor_scalar(
            den_sc[0:1, 0:512], sh_row[0:1, th * 512:(th + 1) * 512],
            swsc[0:1, 2:3], None, ALU.mult)),
            deps=(sh_ld, p1_sc, bc_war, dsc_war))
        dsc_war = ()
        bs2 = OP("pe", (lambda e: e.matmul(PB[:, 1536:2048], ones[0:1, :],
                                           den_sc[0:1, 0:512], start=True, stop=True)),
                 deps=(bs1, pd_free[7] if th == 0 else pd_free[0], ev_ones))
        bs3 = OP("ve", (lambda e, th=th: e.tensor_copy(bcast[:, th * 512:(th + 1) * 512],
                                                       PB[:, 1536:2048])), deps=(bs2,))
        bc_war = bs3
        for o2h in range(2):
            mm = None
            for k2 in range(64):
                tb = k2 % 2
                ld = OP("se", (lambda e, k2=k2, o2h=o2h, tb=tb: e.dma_start_transpose(
                    AR1[:, tb * 1024:(tb + 1) * 1024],
                    t_dram[2][o2h * 1024:(o2h + 1) * 1024, k2 * 128:(k2 + 1) * 128])),
                    deps=(td_war[tb],), dma=True)

                def d_mms(e, k2=k2, tb=tb):
                    rr = None
                    for oc in range(8):
                        rr = e.matmul(bank8(oc),
                                      AR1[:, tb * 1024 + oc * 128: tb * 1024 + (oc + 1) * 128],
                                      hqT_ap(k2), start=(k2 == 0), stop=(k2 == 63))
                    return rr
                mm = OP("pe", d_mms,
                        deps=(ld, ev_hqT[k2], bs3) + (tuple(pd_free) if k2 == 0 else ()))
                td_war[tb] = mm
            hq_prev_half_mm = mm
            pd_free = []
            prev_y2 = None
            prev_y3 = None
            for oc in range(8):
                o2c = o2h * 8 + oc
                sgslot = oc % 2
                y1 = OP("ve", (lambda e, oc=oc, th=th: e.tensor_mul(
                    qb[:, 0:512], bank8(oc), bcast[:, th * 512:(th + 1) * 512])),
                    deps=(mm, bs3, prev_y2,
                          qb4_war if (th == 0 and o2h == 0 and oc == 0) else None))
                y2 = OP("se", (lambda e: e.activation(qb[:, 512:1024], qb[:, 0:512], AF.Silu)),
                        deps=(y1, prev_y3))
                y3 = OP("ve", (lambda e, o2c=o2c, th=th, sgslot=sgslot: e.tensor_mul(
                    AR1[:, 8192 + sgslot * 512: 8192 + (sgslot + 1) * 512],
                    qb[:, 512:1024],
                    AR4[:, o2c * 1024 + th * 512: o2c * 1024 + (th + 1) * 512])),
                    deps=(y2, sg_war[sgslot], ev_s if (th == 0 and o2h == 0 and oc < 2) else None,
                          dbg_evs2 if (th == 0 and o2h == 0 and oc < 2) else None))
                prev_y2 = y2
                prev_y3 = y3
                pd_free.append(y1)
                sg = OP("gp", (lambda e, o2c=o2c, th=th, sgslot=sgslot: e.dma_start(
                    out=sg_dram[o2c * 128:(o2c + 1) * 128, th * 512:(th + 1) * 512],
                    in_=AR1[:, 8192 + sgslot * 512: 8192 + (sgslot + 1) * 512])),
                    deps=(y3,), dma=True)
                sg_war[sgslot] = sg
                ev_fin_sg.append(sg)

    ev_out = []
    sgT_war = [ev_fin_sg[-1], ev_fin_sg[-1]]
    xo_war = [hq_prev_half_mm, hq_prev_half_mm]
    of_war = [ev_fin_sg[-2], ev_fin_sg[-1]]
    for tc in range(8):
        sl = tc % 2
        tr = OP("se", (lambda e, tc=tc, sl=sl: e.dma_start_transpose(
            AR0[:, sl * 2048:(sl + 1) * 2048],
            sg_dram[:, tc * 128:(tc + 1) * 128])),
            deps=(ev_fin_sg, sgT_war[sl], prev_hq), dma=True)
        xl = OP("sy", (lambda e, tc=tc, sl=sl: e.dma_start(
            out=AR1f[:, sl * 2048:(sl + 1) * 2048],
            in_=x_own[tc * 128:(tc + 1) * 128, :])),
            deps=(xo_war[sl],) + ((tuple(ev_fin_sg),) if tc < 2 else ()), dma=True)
        ad = OP("ve", (lambda e, sl=sl: e.tensor_add(
            AR1f[:, 4096 + sl * 2048: 4096 + (sl + 1) * 2048],
            AR1f[:, sl * 2048:(sl + 1) * 2048],
            AR0[:, sl * 2048:(sl + 1) * 2048])),
            deps=(tr, xl, of_war[sl]))
        sgT_war[sl] = ad
        xo_war[sl] = ad
        stq = OP("sy", (lambda e, tc=tc, sl=sl: e.dma_start(
            out=out_own[tc * 128:(tc + 1) * 128, :],
            in_=AR1f[:, 4096 + sl * 2048: 4096 + (sl + 1) * 2048])),
            deps=(ad,), dma=True)
        of_war[sl] = stq
        ev_out.append(stq)
    OP("sy", (lambda e: e.nop()), deps=(ev_out,))

    if debug:
        def ddma(nm, src, deps):
            n = src.shape[-1]
            cs = 512 if n % 512 == 0 else n
            return OP("gp", (lambda e: e.dma_start(
                out=dbg[nm].rearrange("a (b c) -> a b c", c=cs),
                in_=src.rearrange("a (b c) -> a b c", c=cs))), deps=deps, dma=True)
        ddma("d_xn", xn_dram, (ev_xn_done,))
        ddma("d_xq", xq_dram, (ev_xq_done,))
        ddma("d_hq", hq_dram, (ev_hq,))
        ddma("d_sgd", sg_dram, (ev_fin_sg,))
        ddma("d_tg", t_dram[0], (tern["tfree"],))
        ddma("d_td", t_dram[2], (tern["tfree"],))

    block = es.enter_context(nc.Block())

    @block.sync
    def _(e):
        p.emit_engine("sy", e)

    @block.gpsimd
    def _(e):
        p.emit_engine("gp", e)

    @block.tensor
    def _(e):
        p.emit_engine("pe", e)

    @block.vector
    def _(e):
        p.emit_engine("ve", e)

    @block.scalar
    def _(e):
        p.emit_engine("se", e)

    es.close()
    return nc


_PERM_SRC = None


def _perm_src():
    global _PERM_SRC
    if _PERM_SRC is None:
        pm = np.empty(2048, np.int64)
        for s_ in range(2048):
            j, rem = divmod(s_, 512)
            r, rem2 = divmod(rem, 128)
            pp, d = divmod(rem2, 64)
            pm[s_] = (2 * j + pp) * 256 + r * 64 + d
        _PERM_SRC = pm
    return _PERM_SRC


_NC_CACHE = None


def kernel(L0_attn_x, L0_attn_w_qkv, L0_attn_b_qkv, L0_attn_w_out, L0_attn_b_out,
           L0_ffn_w_gate, L0_ffn_w_up, L0_ffn_w_down):
    global _NC_CACHE
    x = np.asarray(L0_attn_x, np.float32)
    perm = _perm_src()
    wqkv_p = np.ascontiguousarray(
        np.concatenate([np.asarray(L0_attn_w_qkv, np.float32)[:2048][perm],
                        np.asarray(L0_attn_w_qkv, np.float32)[2048:]], axis=0))
    bqkv_p = np.ascontiguousarray(
        np.concatenate([np.asarray(L0_attn_b_qkv, np.float32)[:2048][perm],
                        np.asarray(L0_attn_b_qkv, np.float32)[2048:]], axis=0))
    wout_p = np.ascontiguousarray(np.asarray(L0_attn_w_out, np.float32)[:, perm])
    bout = np.ascontiguousarray(np.asarray(L0_attn_b_out, np.float32))
    wg = np.ascontiguousarray(np.asarray(L0_ffn_w_gate, np.float32))
    wu = np.ascontiguousarray(np.asarray(L0_ffn_w_up, np.float32))
    wd = np.ascontiguousarray(np.asarray(L0_ffn_w_down, np.float32))

    if _NC_CACHE is None:
        _NC_CACHE = build()
    nc = _NC_CACHE
    xb = [np.ascontiguousarray(x[0]), np.ascontiguousarray(x[1])]
    in_maps = []
    for c in range(8):
        b, t0 = c // 4, (c % 4) * 1024
        in_maps.append({
            "x_batch": xb[b],
            "x_own": np.ascontiguousarray(x[b, t0:t0 + 1024]),
            "w_qkv": wqkv_p, "b_qkv": bqkv_p, "w_out": wout_p, "b_out": bout,
            "w_gate": wg, "w_up": wu, "w_down": wd,
        })
    res = run_bass_kernel_spmd(nc, in_maps, core_ids=list(range(8)))
    out = np.empty((2, 4096, 2048), np.float32)
    for c in range(8):
        b, t0 = c // 4, (c % 4) * 1024
        out[b, t0:t0 + 1024] = res.results[c]["out_own"]
    return out


# version 05 (final)
# speedup vs baseline: 1.0851x; 1.0851x over previous
import contextlib
import numpy as np
import ml_dtypes
import concourse.bass as bass
import concourse.mybir as mybir
from concourse.bass_utils import run_bass_kernel_spmd

F32 = mybir.dt.float32
F16 = mybir.dt.float16
BF16 = mybir.dt.bfloat16
U32 = mybir.dt.uint32
AF = mybir.ActivationFunctionType
ALU = mybir.AluOpType
AX = mybir.AxisListType

D = 2048
DFF = 8192
S = 4096
TOWN = 1024
MAGIC = 98304.0
MAGICH = MAGIC + 0.5
EPSQ = 1e-5
INV127 = 1.0 / 127.0

DEBUG = False


class Prog:
    ENG = ("sy", "gp", "pe", "ve", "se")
    DMAQ = ("sy", "gp", "se")

    def __init__(self, nc, es):
        self.nc = nc
        self.ops = {k: [] for k in self.ENG}
        self.cnt = {k: 0 for k in self.ENG}
        self.dcnt = {k: 0 for k in self.DMAQ}
        self.sems = {}
        for k in self.ENG:
            self.sems["E" + k] = es.enter_context(nc.semaphore("E" + k))
        for k in self.DMAQ:
            self.sems["D" + k] = es.enter_context(nc.semaphore("D" + k))

    def op(self, eng, emit, deps=(), dma=False):
        flat = []

        def _fl(d):
            if d is None:
                return
            if isinstance(d, tuple) and len(d) == 3 and isinstance(d[0], str):
                flat.append(d)
            elif isinstance(d, (list, tuple)):
                for x in d:
                    _fl(x)
            else:
                raise TypeError(f"bad dep {d!r}")
        _fl(deps)
        if dma:
            assert eng in self.DMAQ
            self.dcnt[eng] += 16
            ev = ("D" + eng, self.dcnt[eng], 16)
        else:
            self.cnt[eng] += 1
            ev = ("E" + eng, self.cnt[eng], 1)
        self.ops[eng].append((tuple(flat), emit, ev))
        return ev

    def emit_engine(self, key, eobj):
        hw = {}
        for deps, emit, ev in self.ops[key]:
            need = {}
            for sn, val, _ in deps:
                if sn == "E" + key and key == "pe":
                    continue
                if need.get(sn, 0) < val:
                    need[sn] = val
            for sn, val in need.items():
                if hw.get(sn, 0) < val:
                    eobj.wait_ge(self.sems[sn], val)
                    hw[sn] = val
            inst = emit(eobj)
            sn, val, inc = ev
            inst.then_inc(self.sems[sn], inc)


def build(debug=DEBUG):
    nc = bass.Bass("TRN2", target_bir_lowering=False)
    es = contextlib.ExitStack()

    dt_ = nc.dram_tensor
    x_batch = dt_("x_batch", [S, D], F32, kind="ExternalInput").ap()
    x_own = dt_("x_own", [TOWN, D], F32, kind="ExternalInput").ap()
    w_qkv = dt_("w_qkv", [3072, D], F32, kind="ExternalInput").ap()
    b_qkv = dt_("b_qkv", [3072], F32, kind="ExternalInput").ap()
    w_out = dt_("w_out", [D, D], F32, kind="ExternalInput").ap()
    b_out = dt_("b_out", [D], F32, kind="ExternalInput").ap()
    w_gate = dt_("w_gate", [DFF, D], F32, kind="ExternalInput").ap()
    w_up = dt_("w_up", [DFF, D], F32, kind="ExternalInput").ap()
    w_down = dt_("w_down", [D, DFF], F32, kind="ExternalInput").ap()
    out_own = dt_("out_own", [TOWN, D], F32, kind="ExternalOutput").ap()

    wqkv_bf = dt_("wqkv_bf", [3072, D], BF16).ap()
    wout_bf = dt_("wout_bf", [D, D], BF16).ap()
    xn_dram = dt_("xn_dram", [S, D], BF16).ap()
    xnown_dram = dt_("xnown_dram", [TOWN, D], BF16).ap()
    xq_dram = dt_("xq_dram", [TOWN, D], BF16).ap()
    t_dram = [dt_("tg_dram", [DFF, D], BF16).ap(),
              dt_("tu_dram", [DFF, D], BF16).ap(),
              dt_("td_dram", [D, DFF], BF16).ap()]
    h16_dram = dt_("h16_dram", [TOWN, DFF], F32).ap()
    hq_dram = dt_("hq_dram", [TOWN, DFF], BF16).ap()
    sg_dram = dt_("sg_dram", [D, TOWN], BF16).ap()
    sh_dram = dt_("sh_dram", [TOWN, 1], F32).ap()
    qhat_dram = dt_("qhat_dram", [2080, 1024], BF16).ap()
    den_dram = dt_("den_dram", [64, 512], F32).ap()
    denr_dram = dt_("denr_dram", [64, 512], F32).ap()
    ot_dram = dt_("ot_dram", [128, 16384], BF16).ap()
    sw_dram = dt_("sw_dram", [1, 8], F32).ap()

    a = nc.alloc_sbuf_tensor
    AR0 = a("AR0", [128, 16384], BF16)
    AR1 = a("AR1", [128, 16384], BF16)
    AR2 = a("AR2", [128, 16384], BF16)
    AR3 = a("AR3", [128, 16384], BF16)
    AR4 = a("AR4", [128, 16640], BF16)
    wt = a("wt", [128, 4096], F32)
    tmp_le = a("tmp_le", [128, 2048], F32)
    ttile = a("ttile", [128, 2048], BF16)
    ones = a("ones", [128, 128], F32)
    bk = a("bk", [128, 4], F32)
    bv = a("bv", [1, 512], F32)
    bq_eff = a("bq_eff", [128, 16], F32)
    bout_t = a("bout_t", [128, 16], F32)
    sx_all = a("sx_all", [128, 8], F32)
    sxg = a("sxg", [128, 8], F32)
    sxu = a("sxu", [128, 8], F32)
    shmax = a("shmax", [128, 8], F32)
    shsc = a("shsc", [128, 8], F32)
    thr = a("thr", [128, 8], F32)
    swsc = a("swsc", [1, 8], F32)
    p1acc = a("p1acc", [128, 4], F32)
    bnst = a("bnst", [128, 24], F32)
    mv = a("mv", [128, 2], F32)
    rstd = a("rstd", [128, 1], F32)
    o_un = a("o_un", [64, 1024], BF16)
    gu = a("gu", [128, 2048], F32)
    qb = a("qb", [128, 1024], F32)
    mrow = a("mrow", [1, 160], F32)
    cmag = a("cmag", [128, 1], F32)

    PA = nc.alloc_psum_tensor("PA", [128, 2048], F32)
    PB = nc.alloc_psum_tensor("PB", [128, 2048], F32)

    def bankA(i):
        return PA[:, i * 512:(i + 1) * 512]

    def bankB(i):
        return PB[:, i * 512:(i + 1) * 512]

    def bank8(i):
        return bankA(i) if i < 4 else bankB(i - 4)

    AR0f = AR0[:].bitcast(F32)
    AR1f = AR1[:].bitcast(F32)
    AR2f = AR2[:].bitcast(F32)
    AR3f = AR3[:].bitcast(F32)
    AR3h = AR3[:].bitcast(F16)
    gub = gu[:].bitcast(BF16)

    den_sc = qb[0:1, 0:1024]
    bcast = gu[:, 0:1024]
    sh_row = gu[0:1, 1024:2048]
    p = Prog(nc, es)
    OP = p.op

    dbg = {}
    if debug:
        for nm, shp, dty in [("d_kt", [128, 16384], BF16), ("d_vt", [128, 16640], BF16),
                             ("d_qt", [2080, 1024], BF16), ("d_ot", [128, 16384], BF16),
                             ("d_s", [128, 16384], BF16), ("d_sw", [1, 8], F32),
                             ("d_sx", [128, 8], F32),
                             ("d_shsc", [128, 8], F32), ("d_xn", [S, D], BF16),
                             ("d_xq", [TOWN, D], BF16), ("d_hq", [TOWN, DFF], BF16),
                             ("d_h16", [TOWN, DFF], F32),
                             ("d_sgd", [D, TOWN], BF16), ("d_tg", [DFF, D], BF16),
                             ("d_td", [D, DFF], BF16)]:
            dbg[nm] = dt_(nm, shp, dty, kind="ExternalOutput").ap()

    ev_ones = OP("ve", lambda e: e.memset(ones[:, :], 1.0))
    ev_cmag = OP("ve", lambda e: e.memset(cmag[:, :], -MAGIC))
    vt1 = AR4[:, :].rearrange("p (i g d) -> p i g d", g=8, d=65)
    ev_vt_ones = OP("ve", lambda e: e.memset(vt1[:, :, :, 64:65], 1.0))

    ev_b = []
    for c in range(4):
        ev_b.append(OP("sy", (lambda e, c=c: e.dma_start(
            out=bk[:, c:c + 1],
            in_=b_qkv[2048 + c * 128:2048 + (c + 1) * 128].rearrange("(a b) -> a b", b=1))), dma=True))
    ev_b.append(OP("sy", lambda e: e.dma_start(
        out=bv[:, :], in_=b_qkv[2560:3072].rearrange("(a b) -> a b", a=1)), dma=True))
    ev_bq_raw = []
    for c in range(16):
        ev_bq_raw.append(OP("sy", (lambda e, c=c: e.dma_start(
            out=bq_eff[:, c:c + 1],
            in_=b_qkv[c * 128:(c + 1) * 128].rearrange("(a b) -> a b", b=1))), dma=True))
        ev_b.append(OP("sy", (lambda e, c=c: e.dma_start(
            out=bout_t[:, c:c + 1],
            in_=b_out[c * 128:(c + 1) * 128].rearrange("(a b) -> a b", b=1))), dma=True))
    ev_bq = OP("ve", lambda e: e.tensor_scalar(bq_eff[:, :], bq_eff[:, :], 0.125, None, ALU.mult),
               deps=ev_bq_raw)

    ev_cast_qkv = []
    for i in range(12):
        ev_cast_qkv.append(OP("gp", (lambda e, i=i: e.dma_start(
            out=wqkv_bf[i * 256:(i + 1) * 256, :].rearrange("a (b c) -> a b c", c=512),
            in_=w_qkv[i * 256:(i + 1) * 256, :].rearrange("a (b c) -> a b c", c=512))), dma=True))
    ev_cast_out = []
    for i in range(8):
        ev_cast_out.append(OP("gp", (lambda e, i=i: e.dma_start(
            out=wout_bf[i * 256:(i + 1) * 256, :].rearrange("a (b c) -> a b c", c=512),
            in_=w_out[i * 256:(i + 1) * 256, :].rearrange("a (b c) -> a b c", c=512))), dma=True))

    TERN = ([(0, rb, 0) for rb in range(DFF // 128)] +
            [(1, rb, 0) for rb in range(DFF // 128)] +
            [(2, rb, cb) for rb in range(D // 128) for cb in range(4)])

    def tsrc(idx):
        w, rb, cb = TERN[idx]
        wsrc = (w_gate, w_up, w_down)[w]
        if w < 2:
            return wsrc[rb * 128:(rb + 1) * 128, :]
        return wsrc[rb * 128:(rb + 1) * 128, cb * 2048:(cb + 1) * 2048]

    def tdst(idx):
        w, rb, cb = TERN[idx]
        dd = t_dram[w]
        if w < 2:
            return dd[rb * 128:(rb + 1) * 128, :]
        return dd[rb * 128:(rb + 1) * 128, cb * 2048:(cb + 1) * 2048]

    ev_p1z = OP("ve", lambda e: e.memset(p1acc[:, :], 0.0))
    p1 = {"i": 0, "lastacc": [ev_p1z, ev_p1z, ev_p1z], "wfree": [None, None]}

    def emit_p1_tiles(n):
        for _ in range(n):
            i = p1["i"]
            if i >= len(TERN):
                return
            p1["i"] = i + 1
            w = TERN[i][0]
            buf = i % 2
            ld = OP("gp", (lambda e, i=i, buf=buf: e.dma_start(
                out=wt[:, buf * 2048:(buf + 1) * 2048], in_=tsrc(i))),
                deps=(p1["wfree"][buf],), dma=True)
            acc = OP("ve", (lambda e, buf=buf: e.scalar_tensor_tensor(
                tmp_le[:, :], wt[:, buf * 2048:(buf + 1) * 2048], -1.0,
                wt[:, buf * 2048:(buf + 1) * 2048], ALU.mult, ALU.max,
                accum_out=qb[:, buf:buf + 1])),
                deps=(ld,))
            acc2 = OP("ve", (lambda e, w=w, buf=buf: e.tensor_add(
                p1acc[:, w:w + 1], p1acc[:, w:w + 1], qb[:, buf:buf + 1])),
                deps=(acc, p1["lastacc"][w]))
            p1["lastacc"][w] = acc2
            p1["wfree"][buf] = acc

    ln_free = [ev_p1z, ev_p1z]
    xn_war = None
    qz_war = None
    rstd_war = None
    ev_xn_done = []
    ev_xq_done = []
    ev_sx = []
    for i in range(40):
        buf = i % 2
        src = (x_batch[i * 128:(i + 1) * 128, :] if i < 32 else
               x_own[(i - 32) * 128:(i - 31) * 128, :])
        ld = OP("sy", (lambda e, src=src, buf=buf: e.dma_start(
            out=AR0f[:, buf * 2048:(buf + 1) * 2048], in_=src)),
            deps=(ln_free[buf],), dma=True)
        st = None
        for k in range(4):
            st = OP("ve", (lambda e, buf=buf, k=k: e.bn_stats(
                bnst[:, k * 6:(k + 1) * 6],
                AR0f[:, buf * 2048 + k * 512: buf * 2048 + (k + 1) * 512])),
                deps=(ld,))
        ag = OP("ve", (lambda e: e.bn_aggr(mv[:, :], bnst[:, :])), deps=(st,))
        t1 = OP("ve", (lambda e: e.tensor_scalar(rstd[:, :], mv[:, 1:2], 1e-5, None, ALU.add)),
                deps=(ag, rstd_war))
        t2 = OP("se", (lambda e: e.activation(rstd[:, :], rstd[:, :], AF.Sqrt)), deps=(t1,))
        t3 = OP("ve", (lambda e: e.reciprocal(rstd[:, :], rstd[:, :])), deps=(t2,))
        rstd_war = t3
        nm = OP("ve", (lambda e, buf=buf: e.tensor_scalar(
            AR0f[:, 4096:6144], AR0f[:, buf * 2048:(buf + 1) * 2048],
            mv[:, 0:1], rstd[:, 0:1], ALU.subtract, ALU.mult)),
            deps=(t3, xn_war))
        ln_free[buf] = nm
        dst = (xn_dram[i * 128:(i + 1) * 128, :] if i < 32 else
               xnown_dram[(i - 32) * 128:(i - 31) * 128, :])
        cd = OP("gp", (lambda e, dst=dst: e.dma_start(
            out=dst.rearrange("a (b c) -> a b c", c=512),
            in_=AR0f[:, 4096:6144].rearrange("a (b c) -> a b c", c=512))),
            deps=(nm,), dma=True)
        xn_war = cd
        ev_xn_done.append(cd)
        if i >= 32:
            j = i - 32
            mx = OP("ve", (lambda e: e.reduce_max(rstd[:, :], AR0f[:, 4096:6144],
                                                  axis=AX.X, apply_absolute_value=True)),
                    deps=(nm, t3))
            rstd_war = mx
            mx2 = OP("ve", (lambda e: e.tensor_scalar(rstd[:, :], rstd[:, :], EPSQ, INV127,
                                                      ALU.max, ALU.mult)), deps=(mx,))
            sx = OP("ve", (lambda e, j=j: e.tensor_copy(sx_all[:, j:j + 1], rstd[:, :])),
                    deps=(mx2,))
            ev_sx.append(sx)
            rr = OP("ve", (lambda e: e.reciprocal(rstd[:, :], rstd[:, :])), deps=(sx,))
            rstd_war = rr
            q1 = OP("ve", (lambda e: e.tensor_scalar(
                AR0f[:, 6144:8192], AR0f[:, 4096:6144], rstd[:, 0:1], MAGICH,
                ALU.mult, ALU.add)), deps=(rr, cd))
            xn_war = q1
            q2 = OP("ve", (lambda e: e.tensor_scalar(
                AR0f[:, 6144:8192].bitcast(U32), AR0f[:, 6144:8192].bitcast(U32),
                0xFFFFFF80, None, ALU.bitwise_and)), deps=(q1,))
            q3 = OP("ve", (lambda e: e.tensor_scalar(
                gub[:, 0:2048], AR0f[:, 6144:8192], -MAGIC, None, ALU.add)),
                deps=(q2, qz_war))
            qd = OP("gp", (lambda e, j=j: e.dma_start(
                out=xq_dram[j * 128:(j + 1) * 128, :], in_=gub[:, 0:2048])),
                deps=(q3,), dma=True)
            qz_war = qd
            ev_xq_done.append(qd)
        emit_p1_tiles(2)

    ev_wkvT = []
    for c in range(16):
        ev_wkvT.append(OP("sy", (lambda e, c=c: e.dma_start_transpose(
            AR2[:, c * 1024:(c + 1) * 1024],
            wqkv_bf[2048:3072, c * 128:(c + 1) * 128])),
            deps=ev_cast_qkv, dma=True))

    xnT_war = [None, None]
    kbank_free = [None] * 4
    vbank_free = [None] * 4
    ev_K_ev = []
    ev_V_ev = []
    for tt in range(8):
        half = tt % 2
        base = half * 8192
        ev_xt = []
        for c in range(16):
            ev_xt.append(OP("sy", (lambda e, c=c, tt=tt, base=base: e.dma_start_transpose(
                AR1[:, base + c * 512: base + (c + 1) * 512],
                xn_dram[tt * 512:(tt + 1) * 512, c * 128:(c + 1) * 128])),
                deps=(ev_xn_done[tt * 4 + 3], xnT_war[half]), dma=True))

        for m in range(4):
            def k_mms(e, m=m, base=base):
                r = None
                for k in range(16):
                    r = e.matmul(bankA(m),
                                 AR2[:, k * 1024 + m * 128: k * 1024 + (m + 1) * 128],
                                 AR1[:, base + k * 512: base + (k + 1) * 512],
                                 start=(k == 0), stop=(k == 15))
                return r
            mm = OP("pe", k_mms, deps=(ev_wkvT, ev_xt, kbank_free[m]))
            evc = OP("se", (lambda e, m=m, tt=tt: e.activation(
                AR3[:, m * 4096 + tt * 512: m * 4096 + (tt + 1) * 512],
                bankA(m), AF.Identity, bias=bk[:, m:m + 1], scale=1.0)),
                deps=(mm, ev_b))
            kbank_free[m] = evc
            ev_K_ev.append(evc)

        for m in range(4):
            def v_mms(e, m=m, base=base):
                r = e.matmul(bankB(m), ones[0:1, 0:128], bv[0:1, :], start=True, stop=False)
                for k in range(16):
                    r = e.matmul(bankB(m),
                                 AR1[:, base + k * 512 + m * 128: base + k * 512 + (m + 1) * 128],
                                 AR2[:, k * 1024 + 512: (k + 1) * 1024],
                                 start=False, stop=(k == 15))
                return r
            mm = OP("pe", v_mms, deps=(ev_wkvT, ev_xt, vbank_free[m], ev_ones, ev_b))
            vchunk = tt * 4 + m
            vdst = AR4[:, vchunk * 520: vchunk * 520 + 520].rearrange(
                "p (g d) -> p g d", g=8)[:, :, 0:64]
            vsrc = PB[:, m * 512:(m + 1) * 512].rearrange("p (g d) -> p g d", g=8)
            evc = OP("ve", (lambda e, vdst=vdst, vsrc=vsrc: e.tensor_copy(vdst, vsrc)),
                     deps=(mm, ev_vt_ones))
            vbank_free[m] = evc
            ev_V_ev.append(evc)
        xnT_war[half] = ev_V_ev[-1]
        emit_p1_tiles(14)
    emit_p1_tiles(len(TERN))

    tern = {"i": 0, "free": [None, None], "tfree": [None, None]}

    def emit_tern_tiles(n):
        for _ in range(n):
            i = tern["i"]
            if i >= len(TERN):
                return
            tern["i"] = i + 1
            w = TERN[i][0]
            buf = i % 2
            ld = OP("gp", (lambda e, i=i, buf=buf: e.dma_start(
                out=wt[:, buf * 2048:(buf + 1) * 2048], in_=tsrc(i))),
                deps=(tern["free"][buf], thr_p, thr_n, p1["wfree"][buf]), dma=True)
            c1 = OP("ve", (lambda e, buf=buf, w=w: e.tensor_scalar(
                tmp_le[:, :], wt[:, buf * 2048:(buf + 1) * 2048],
                thr[:, 4 + w:5 + w], None, ALU.is_le)), deps=(ld,))
            c2 = OP("ve", (lambda e, buf=buf, w=w: e.scalar_tensor_tensor(
                ttile[:, 0:2048],
                wt[:, buf * 2048:(buf + 1) * 2048], thr[:, w:w + 1],
                tmp_le[:, :], ALU.is_ge, ALU.subtract)),
                deps=(c1, tern["tfree"][0], tern["tfree"][1]))
            tern["free"][buf] = c2
            stq = OP("gp", (lambda e, i=i: e.dma_start(
                out=tdst(i), in_=ttile[:, 0:2048])),
                deps=(c2,), dma=True)
            tern["tfree"][0] = stq

    ev_xnownT = []
    for c in range(16):
        ev_xnownT.append(OP("sy", (lambda e, c=c: e.dma_start_transpose(
            AR2[:, c * 1024:(c + 1) * 1024],
            xnown_dram[:, c * 128:(c + 1) * 128])),
            deps=(ev_xn_done[39], ev_K_ev, ev_V_ev), dma=True))

    ev_QT = []
    b8_free = kbank_free + vbank_free
    wq_war = [xnT_war[0], xnT_war[1]]
    for qg in range(4):
        half = qg % 2
        base = half * 8192
        ev_wq = []
        for k in range(16):
            ev_wq.append(OP("sy", (lambda e, qg=qg, k=k, base=base: e.dma_start_transpose(
                AR1[:, base + k * 512: base + (k + 1) * 512],
                wqkv_bf[qg * 512:(qg + 1) * 512, k * 128:(k + 1) * 128])),
                deps=(ev_cast_qkv, wq_war[half]), dma=True))

        def q_mms(e, base=base):
            r = None
            for qc in range(4):
                for nn in range(2):
                    for k in range(16):
                        r = e.matmul(bank8(qc * 2 + nn),
                                     AR1[:, base + k * 512 + qc * 128: base + k * 512 + (qc + 1) * 128],
                                     AR2[:, k * 1024 + nn * 512: k * 1024 + (nn + 1) * 512],
                                     start=(k == 0), stop=(k == 15))
            return r
        mm = OP("pe", q_mms, deps=(ev_wq, ev_xnownT, b8_free))
        evs = []
        for qc in range(4):
            for nn in range(2):
                c = qg * 4 + qc
                evs.append(OP("se", (lambda e, qc=qc, nn=nn, c=c: e.activation(
                    AR0[:, c * 1024 + nn * 512: c * 1024 + (nn + 1) * 512],
                    bank8(qc * 2 + nn), AF.Identity, bias=bq_eff[:, c:c + 1], scale=0.125)),
                    deps=(mm, ev_bq, ev_xq_done[-1] if qg == 0 else None)))
        b8_free = evs
        wq_war[half] = evs[-1]
        ev_QT.extend(evs)

    ev_bparts = []
    sq_war = tuple(p1["lastacc"])
    red_war = None
    pa01_war = [b8_free, b8_free]
    last_sqr = None
    for g in range(8):
        j, p2 = g // 2, g % 2
        sq = OP("se", (lambda e, j=j, p2=p2: e.activation(
            AR1f[0:64, 0:4096], AR3[p2 * 64:(p2 + 1) * 64, j * 4096:(j + 1) * 4096],
            AF.Square)),
            deps=(ev_K_ev, sq_war, wq_war[0], wq_war[1]))
        parts = []
        for cc in range(8):
            s1 = OP("pe", (lambda e, cc=cc: e.matmul(
                PA[0:1, (cc % 2) * 512:(cc % 2 + 1) * 512], ones[0:64, 0:1],
                AR1f[0:64, cc * 512:(cc + 1) * 512], start=True, stop=True)),
                deps=(sq, pa01_war[cc % 2], ev_ones))
            r1 = OP("ve", (lambda e, g=g, cc=cc: e.reduce_max(
                mrow[0:1, 80 + cc:81 + cc], PA[0:1, (cc % 2) * 512:(cc % 2 + 1) * 512],
                axis=AX.X)), deps=(s1, red_war))
            pa01_war[cc % 2] = r1
            red_war = r1
            parts.append(r1)
        km = OP("ve", (lambda e, g=g: e.reduce_max(
            mrow[0:1, 64 + g:65 + g], mrow[0:1, 80:88], axis=AX.X)), deps=(parts,))
        red_war = km
        sq_war = km
        ev_bparts.append(km)
    kmx = OP("se", (lambda e: e.activation(mrow[0:1, 72:80], mrow[0:1, 64:72], AF.Sqrt)),
             deps=(ev_bparts,))
    ev_bparts.append(kmx)

    qhat_exports = []
    mq_war = None
    for c in range(16):
        sq2 = OP("se", (lambda e, c=c: e.activation(
            qb[:, :], AR0[:, c * 1024:(c + 1) * 1024], AF.Square)),
            deps=(ev_QT[c * 2], ev_QT[c * 2 + 1], kmx, sq_war))
        sq_war = sq2
        for p2 in range(2):
            h = c * 2 + p2
            g = 2 * (c // 4) + p2
            s1 = OP("pe", (lambda e, p2=p2: e.matmul(
                PA[0:1, 0:512], ones[p2 * 64:(p2 + 1) * 64, 0:1],
                qb[p2 * 64:(p2 + 1) * 64, 0:512], start=True, stop=True)),
                deps=(sq2, pa01_war[0], ev_ones))
            s2 = OP("pe", (lambda e, p2=p2: e.matmul(
                PA[0:1, 512:1024], ones[p2 * 64:(p2 + 1) * 64, 0:1],
                qb[p2 * 64:(p2 + 1) * 64, 512:1024], start=True, stop=True)),
                deps=(sq2, pa01_war[1], ev_ones))
            sqr = OP("se", (lambda e: e.activation(gu[0:1, 0:1024], PA[0:1, 0:1024],
                                                   AF.Sqrt)),
                     deps=(s1, s2, mq_war, ev_xq_done))
            pa01_war = [sqr, sqr]
            mqr = OP("ve", (lambda e, g=g: e.tensor_scalar(
                gub[0:1, 2048:3072], gu[0:1, 0:1024],
                mrow[0:1, 72 + g:73 + g], -1.0, ALU.mult, ALU.mult)),
                deps=(sqr, kmx))
            mq_war = mqr
            qd1 = OP("gp", (lambda e, h=h: e.dma_start(
                out=qhat_dram[h * 65 + 64:h * 65 + 65, :], in_=gub[0:1, 2048:3072])),
                deps=(mqr,), dma=True)
            qd2 = OP("gp", (lambda e, h=h, c=c, p2=p2: e.dma_start(
                out=qhat_dram[h * 65:h * 65 + 64, :],
                in_=AR0[p2 * 64:(p2 + 1) * 64, c * 1024:(c + 1) * 1024])),
                deps=(ev_QT[c * 2], ev_QT[c * 2 + 1]), dma=True)
            qhat_exports.extend([qd1, qd2])
            mq_war = qd1

    ev_conv = []
    for j in range(4):
        cpo = OP("ve", (lambda e, j=j: e.tensor_copy(
            AR0[0:64, j * 4096:(j + 1) * 4096],
            AR3[64:128, j * 4096:(j + 1) * 4096])),
            deps=(qhat_exports, ev_bparts, red_war))
        m1 = OP("ve", (lambda e, j=j: e.memset(
            AR3[64:65, j * 4096:(j + 1) * 4096], 1.0)), deps=(cpo,))
        m2 = OP("ve", (lambda e, j=j: e.memset(
            AR0[64:65, j * 4096:(j + 1) * 4096], 1.0)), deps=(cpo,))
        ev_conv.extend([cpo, m1, m2])

    p1_sum = OP("pe", (lambda e: e.matmul(PB[0:1, 0:3], ones[:, 0:1], p1acc[:, 0:3],
                                          start=True, stop=True)),
                deps=(p1["lastacc"], ev_QT, ev_ones))
    p1_sc = OP("ve", (lambda e: e.tensor_scalar(swsc[0:1, 0:3], PB[0:1, 0:3],
                                                1.0 / (DFF * D), EPSQ, ALU.mult, ALU.add)),
               deps=(p1_sum,))
    sw_st = OP("gp", (lambda e: e.dma_start(out=sw_dram, in_=swsc[0:1, :])),
               deps=(p1_sc,), dma=True)
    sw_ld = OP("sy", (lambda e: e.dma_start(out=swsc[0:1, :], in_=sw_dram)),
               deps=(sw_st,), dma=True)
    thr_mm = OP("pe", (lambda e: e.matmul(PB[:, 1024:1027], ones[0:1, :], swsc[0:1, 0:3],
                                          start=True, stop=True)),
                deps=(sw_ld, ev_QT, ev_ones))
    thr_p = OP("ve", (lambda e: e.tensor_scalar(thr[:, 0:3], PB[:, 1024:1027], 0.5, None, ALU.mult)),
               deps=(thr_mm,))
    thr_n = OP("ve", (lambda e: e.tensor_scalar(thr[:, 4:7], PB[:, 1024:1027], -0.5, None, ALU.mult)),
               deps=(thr_mm,))

    def khat(g):
        ar = AR3 if g % 2 == 0 else AR0
        return ar[:, (g // 2) * 4096:(g // 2 + 1) * 4096]

    ev_attn_done = []
    ev_den = []
    den_free = [thr_n, thr_n]
    pa_war2 = [pa01_war[0], pa01_war[1]]
    av_war = [thr_n, thr_n]
    pb2_war = thr_n
    pslot_war = [wq_war[0], wq_war[1]]
    qbuf_war = [red_war, red_war]
    for h in range(32):
        c, p2 = h // 2, h % 2
        j, r = c // 4, c % 4
        g = 2 * j + p2
        qs = h % 2
        qld = OP("sy", (lambda e, h=h, qs=qs: e.dma_start(
            out=AR1[0:65, 2048 + qs * 1024: 2048 + qs * 1024 + 1024],
            in_=qhat_dram[h * 65:(h + 1) * 65, :])),
            deps=(qhat_exports[2 * h], qhat_exports[2 * h + 1], qbuf_war[qs]), dma=True)
        for qt in range(2):
            bank = (h * 2 + qt) % 2
            av_last = None
            for sc in range(16):
                slot = sc % 2

                def sc_mms(e, g=g, qs=qs, qt=qt, sc=sc, slot=slot):
                    rr = None
                    for kk in range(2):
                        kchunk = sc * 2 + kk
                        rr = e.matmul(
                            PA[:, slot * 1024 + kk * 512: slot * 1024 + (kk + 1) * 512],
                            khat(g)[0:65, kchunk * 128:(kchunk + 1) * 128],
                            AR1[0:65, 2048 + qs * 1024 + qt * 512:
                                2048 + qs * 1024 + (qt + 1) * 512],
                            start=True, stop=True)
                    return rr
                mm = OP("pe", sc_mms, deps=(pa_war2[slot], qld, ev_conv))
                ex = OP("se", (lambda e, slot=slot: e.activation(
                    AR1[:, slot * 1024:(slot + 1) * 1024],
                    PA[:, slot * 1024:(slot + 1) * 1024], AF.Exp)),
                    deps=(mm, pslot_war[slot]))
                pa_war2[slot] = ex

                def av_mms(e, g=g, sc=sc, slot=slot, bank=bank):
                    rr = None
                    for kk in range(2):
                        kchunk = sc * 2 + kk
                        vbase = kchunk * 520
                        rr = e.matmul(
                            PB[0:65, bank * 512:(bank + 1) * 512],
                            AR4[:, vbase + g * 65: vbase + (g + 1) * 65],
                            AR1[:, slot * 1024 + kk * 512: slot * 1024 + (kk + 1) * 512],
                            start=(sc == 0 and kk == 0), stop=(sc == 15 and kk == 1))
                    return rr
                av = OP("pe", av_mms,
                        deps=(ex,) + ((av_war[bank],) if sc == 0 else ()))
                pslot_war[slot] = av
                av_last = av
            qbuf_war[qs] = av_last
            cp_o = OP("ve", (lambda e, p2=p2, c=c, qt=qt, bank=bank: e.tensor_copy(
                AR2[p2 * 64:(p2 + 1) * 64, c * 1024 + qt * 512: c * 1024 + (qt + 1) * 512],
                PB[0:64, bank * 512:(bank + 1) * 512])),
                deps=(av_last,))
            dsl = (h * 2 + qt) % 2
            cp_d = OP("ve", (lambda e, bank=bank, dsl=dsl: e.tensor_copy(
                den_sc[0:1, dsl * 512:(dsl + 1) * 512],
                PB[64:65, bank * 512:(bank + 1) * 512])),
                deps=(av_last, den_free[dsl]))
            dd = OP("gp", (lambda e, h=h, qt=qt, dsl=dsl: e.dma_start(
                out=den_dram[h * 2 + qt:h * 2 + qt + 1, :],
                in_=den_sc[0:1, dsl * 512:(dsl + 1) * 512])),
                deps=(cp_d,), dma=True)
            den_free[dsl] = dd
            ev_den.append(dd)
            av_war[bank] = cp_o if True else None
            av_war[bank] = (cp_o, cp_d)
            ev_attn_done.append(cp_o)
            emit_tern_tiles(3)
    emit_tern_tiles(len(TERN))

    nrm_evs = []
    r_slot_war = [ev_den[-1], ev_den[-1]]
    bc_war2 = [pb2_war, pb2_war]
    last_r_ld = ev_den[-1]
    for h in range(32):
        c, p2 = h // 2, h % 2
        for qt in range(2):
            u = h * 2 + qt
            rs2 = u % 2
            r_ld = OP("sy", (lambda e, u=u, rs2=rs2: e.dma_start(
                out=gu[0:1, rs2 * 512:(rs2 + 1) * 512],
                in_=den_dram[u:u + 1, :])),
                deps=(ev_den[u], r_slot_war[rs2]), dma=True)
            last_r_ld = r_ld
            rc1 = OP("se", (lambda e, rs2=rs2: e.activation(
                gu[0:1, rs2 * 512:(rs2 + 1) * 512],
                gu[0:1, rs2 * 512:(rs2 + 1) * 512], AF.Ln)), deps=(r_ld,))
            rc2 = OP("se", (lambda e, rs2=rs2: e.activation(
                gu[0:1, rs2 * 512:(rs2 + 1) * 512],
                gu[0:1, rs2 * 512:(rs2 + 1) * 512], AF.Exp,
                bias=0.0, scale=-1.0)), deps=(rc1,))
            bcmm = OP("pe", (lambda e, rs2=rs2: e.matmul(
                PB[0:64, 1024 + rs2 * 512:1024 + (rs2 + 1) * 512], ones[0:1, 0:64],
                gu[0:1, rs2 * 512:(rs2 + 1) * 512], start=True, stop=True)),
                deps=(rc2, bc_war2[rs2], ev_ones))
            nrm = OP("ve", (lambda e, p2=p2, c=c, qt=qt, rs2=rs2: e.tensor_mul(
                AR2[p2 * 64:(p2 + 1) * 64, c * 1024 + qt * 512: c * 1024 + (qt + 1) * 512],
                AR2[p2 * 64:(p2 + 1) * 64, c * 1024 + qt * 512: c * 1024 + (qt + 1) * 512],
                PB[0:64, 1024 + rs2 * 512:1024 + (rs2 + 1) * 512])),
                deps=(bcmm, ev_attn_done[u]))
            r_slot_war[rs2] = nrm
            bc_war2[rs2] = nrm
            nrm_evs.append(nrm)
    ot_st = []
    for i in range(4):
        ot_st.append(OP("gp", (lambda e, i=i: e.dma_start(
            out=ot_dram[:, i * 4096:(i + 1) * 4096],
            in_=AR2[:, i * 4096:(i + 1) * 4096])), deps=(nrm_evs,), dma=True))
    ot_ld = []
    for i in range(4):
        ot_ld.append(OP("sy", (lambda e, i=i: e.dma_start(
            out=AR2[:, i * 4096:(i + 1) * 4096],
            in_=ot_dram[:, i * 4096:(i + 1) * 4096])), deps=(ot_st,), dma=True))

    if debug:
        dbg_evs = []
        dbg_evs.append(OP("gp", (lambda e: e.dma_start(out=dbg["d_kt"], in_=AR3[:, :])),
                          deps=(ev_attn_done,), dma=True))
        dbg_evs.append(OP("gp", (lambda e: e.dma_start(out=dbg["d_vt"], in_=AR4[:, :])),
                          deps=(ev_attn_done,), dma=True))
        dbg_evs.append(OP("gp", (lambda e: e.dma_start(
            out=dbg["d_qt"].rearrange("a (b c) -> a b c", c=512),
            in_=qhat_dram.rearrange("a (b c) -> a b c", c=512))),
            deps=(qhat_exports,), dma=True))
        dbg_evs.append(OP("gp", (lambda e: e.dma_start(out=dbg["d_ot"], in_=AR2[:, :])),
                          deps=(ev_attn_done,), dma=True))
        dbg_evs.append(OP("gp", (lambda e: e.dma_start(out=dbg["d_sx"], in_=sx_all[:, :])),
                          deps=(ev_sx,), dma=True))
        dbg_evs.append(OP("gp", (lambda e: e.dma_start(out=dbg["d_sw"], in_=swsc[0:1, :])),
                          deps=(p1_sc, thr_n), dma=True))
    else:
        dbg_evs = []


    ev_s = []
    wo_war = [pslot_war[0], pslot_war[1]]
    b8_free = [pa_war2[0], pa_war2[0], pa_war2[1], pa_war2[1],
               av_war[0], av_war[1], nrm_evs[-2], nrm_evs[-1]]
    s_war = [av_war[0], av_war[1], nrm_evs[-1]]
    for og in range(4):
        half = og % 2
        base = half * 8192
        ev_wo = []
        for k in range(16):
            ev_wo.append(OP("sy", (lambda e, og=og, k=k, base=base: e.dma_start_transpose(
                AR1[:, base + k * 512: base + (k + 1) * 512],
                wout_bf[og * 512:(og + 1) * 512, k * 128:(k + 1) * 128])),
                deps=(ev_cast_out, wo_war[half]), dma=True))

        def o_mms(e, base=base):
            rr = None
            for oc in range(4):
                for nn in range(2):
                    for k in range(16):
                        rr = e.matmul(bank8(oc * 2 + nn),
                                      AR1[:, base + k * 512 + oc * 128: base + k * 512 + (oc + 1) * 128],
                                      AR2[:, k * 1024 + nn * 512: k * 1024 + (nn + 1) * 512],
                                      start=(k == 0), stop=(k == 15))
            return rr
        mm = OP("pe", o_mms, deps=(ev_wo, ot_ld, b8_free, dbg_evs))
        evs = []
        for oc in range(4):
            for nn in range(2):
                c = og * 4 + oc
                evs.append(OP("se", (lambda e, oc=oc, nn=nn, c=c: e.activation(
                    AR4[:, c * 1024 + nn * 512: c * 1024 + (nn + 1) * 512],
                    bank8(oc * 2 + nn), AF.Identity, bias=bout_t[:, c:c + 1], scale=1.0)),
                    deps=(mm, ev_b) + ((tuple(s_war),) if og == 0 else ())))
        b8_free = evs
        wo_war[half] = evs[-1]
        ev_s.extend(evs)

    ev_xqT = []
    for c in range(16):
        ev_xqT.append(OP("sy", (lambda e, c=c: e.dma_start_transpose(
            AR0[:, c * 1024:(c + 1) * 1024],
            xq_dram[:, c * 128:(c + 1) * 128])),
            deps=(ev_xq_done, ev_attn_done, ev_bparts), dma=True))
    swg_b = OP("ve", (lambda e: e.tensor_scalar(qb[:, 0:3], thr[:, 0:3], 2.0, None, ALU.mult)),
               deps=(thr_p, red_war, ev_bparts, ev_attn_done))
    ev_sxg = OP("ve", (lambda e: e.tensor_scalar(sxg[:, :], sx_all[:, :], qb[:, 0:1], None, ALU.mult)),
                deps=(swg_b, ev_sx))
    ev_sxu = OP("ve", (lambda e: e.tensor_scalar(sxu[:, :], sx_all[:, :], qb[:, 1:2], None, ALU.mult)),
                deps=(swg_b, ev_sx))
    ev_shz = OP("ve", (lambda e: e.memset(shmax[:, :], 0.0)))

    ev_h16 = []
    g_war = [wo_war[0], wo_war[1]]
    u_war = [ev_s[-1], ev_s[-1]]
    gu_war = [ev_shz] * 4
    h_war = [None] * 4
    pg_free = ev_s[-2]
    pu_free = ev_s[-1]
    shm_dep = {tc: ev_shz for tc in range(8)}
    last_gu_mm = None
    qb4_war = ev_sxu
    for seg in range(16):
        half = seg % 2
        base = half * 8192
        ev_tg = []
        ev_tu = []
        for k in range(16):
            ev_tg.append(OP("sy", (lambda e, seg=seg, k=k, base=base: e.dma_start_transpose(
                AR1[:, base + k * 512: base + (k + 1) * 512],
                t_dram[0][seg * 512:(seg + 1) * 512, k * 128:(k + 1) * 128])),
                deps=(tern["tfree"], g_war[half]), dma=True))
            ev_tu.append(OP("sy", (lambda e, seg=seg, k=k, base=base: e.dma_start_transpose(
                AR2[:, base + k * 512: base + (k + 1) * 512],
                t_dram[1][seg * 512:(seg + 1) * 512, k * 128:(k + 1) * 128])),
                deps=(tern["tfree"], u_war[half]), dma=True))
        for tc in range(8):
            gslot = tc % 2
            uslot = 2 + tc % 2
            hslot = tc % 4

            def g_mm(e, tc=tc, base=base):
                rr = None
                for k in range(16):
                    rr = e.matmul(bankA(tc % 4),
                                  AR0[:, k * 1024 + tc * 128: k * 1024 + (tc + 1) * 128],
                                  AR1[:, base + k * 512: base + (k + 1) * 512],
                                  start=(k == 0), stop=(k == 15))
                return rr

            def u_mm(e, tc=tc, base=base):
                rr = None
                for k in range(16):
                    rr = e.matmul(bankB(tc % 4),
                                  AR0[:, k * 1024 + tc * 128: k * 1024 + (tc + 1) * 128],
                                  AR2[:, base + k * 512: base + (k + 1) * 512],
                                  start=(k == 0), stop=(k == 15))
                return rr
            mmg = OP("pe", g_mm, deps=(ev_tg, ev_xqT, pg_free))
            mmu = OP("pe", u_mm, deps=(ev_tu, ev_xqT, pu_free))
            ge = OP("se", (lambda e, tc=tc, gslot=gslot: e.activation(
                gu[:, gslot * 512:(gslot + 1) * 512], bankA(tc % 4), AF.Sigmoid,
                bias=0.0, scale=sxg[:, tc:tc + 1])),
                deps=(mmg, ev_sxg, gu_war[gslot]))
            ue = OP("ve", (lambda e, tc=tc, uslot=uslot: e.tensor_scalar(
                gu[:, uslot * 512:(uslot + 1) * 512], bankB(tc % 4),
                sxu[:, tc:tc + 1], None, ALU.mult)),
                deps=(mmu, ev_sxu, gu_war[uslot]))
            pg_free = ge
            pu_free = ue
            pad = OP("ve", (lambda e, gslot=gslot: e.tensor_copy(
                qb[:, 6:7], gu[:, gslot * 512 + 511: gslot * 512 + 512])),
                deps=(ge,))
            hm = OP("ve", (lambda e, gslot=gslot, uslot=uslot, hslot=hslot: e.tensor_mul(
                AR3f[:, hslot * 512:(hslot + 1) * 512],
                gu[:, gslot * 512:(gslot + 1) * 512],
                gu[:, uslot * 512:(uslot + 1) * 512])),
                deps=(ge, pad, ue, h_war[hslot]))
            gu_war[gslot] = hm
            gu_war[uslot] = hm
            mxop = OP("ve", (lambda e, hslot=hslot: e.reduce_max(
                qb[:, 4:5], AR3f[:, hslot * 512:(hslot + 1) * 512], axis=AX.X,
                apply_absolute_value=True)), deps=(hm, qb4_war))
            mxac = OP("ve", (lambda e, tc=tc: e.tensor_max(
                shmax[:, tc:tc + 1], shmax[:, tc:tc + 1], qb[:, 4:5])),
                deps=(mxop, shm_dep[tc]))
            shm_dep[tc] = mxac
            qb4_war = mxac
            hd = OP("gp", (lambda e, seg=seg, tc=tc, hslot=hslot: e.dma_start(
                out=h16_dram[tc * 128:(tc + 1) * 128, seg * 512:(seg + 1) * 512],
                in_=AR3f[:, hslot * 512:(hslot + 1) * 512])),
                deps=(hm, mxac), dma=True)
            h_war[hslot] = hd
            ev_h16.append(hd)
            last_gu_mm = mmu
        g_war[half] = last_gu_mm
        u_war[half] = last_gu_mm

    qb_bf = qb[:].bitcast(BF16)
    stg = [ttile[:, 0:2048], qb_bf[:, 0:2048]]
    ev_hq = {}
    ev_shw = []
    prev_q3 = None
    prev_q1 = None
    prev_hqd = {}
    rstd_war2 = rstd_war
    sdep = {}
    for tc in range(8):
        s1 = OP("ve", (lambda e, tc=tc: e.tensor_scalar(
            shsc[:, tc:tc + 1], shmax[:, tc:tc + 1], EPSQ, INV127, ALU.max, ALU.mult)),
            deps=(shm_dep[tc],))
        sdep[tc] = s1
        ev_shw.append(OP("gp", (lambda e, tc=tc: e.dma_start(
            out=sh_dram[tc * 128:(tc + 1) * 128, :], in_=shsc[:, tc:tc + 1])),
            deps=(s1,), dma=True))
    rq_war = {0: (tern["free"][0], tern["free"][1]), 1: (tern["free"][0], tern["free"][1])}
    st_war = {0: tern["tfree"][0], 1: (qb4_war,)}
    rc_prev = rstd_war2
    for u in range(32):
        tc, q4 = u // 4, u % 4
        rs = u % 2
        ld = OP("sy", (lambda e, tc=tc, q4=q4, rs=rs: e.dma_start(
            out=wt[:, rs * 2048:(rs + 1) * 2048],
            in_=h16_dram[tc * 128:(tc + 1) * 128, q4 * 2048:(q4 + 1) * 2048])),
            deps=tuple(ev_h16[(q4 * 4 + s) * 8 + tc] for s in range(4))
            + (rq_war[rs],), dma=True)
        if q4 == 0:
            rc = OP("ve", (lambda e, tc=tc: e.reciprocal(rstd[:, :], shsc[:, tc:tc + 1])),
                    deps=(sdep[tc], rc_prev, prev_q1))
            rc_prev = rc
        q1 = OP("ve", (lambda e, rs=rs: e.tensor_scalar(
            wt[:, rs * 2048:(rs + 1) * 2048], wt[:, rs * 2048:(rs + 1) * 2048],
            rstd[:, 0:1], MAGICH, ALU.mult, ALU.add)),
            deps=(ld, rc_prev, prev_q1))
        q2 = OP("ve", (lambda e, rs=rs: e.tensor_scalar(
            wt[:, rs * 2048:(rs + 1) * 2048].bitcast(U32),
            wt[:, rs * 2048:(rs + 1) * 2048].bitcast(U32),
            0xFFFFFF80, None, ALU.bitwise_and)), deps=(q1,))
        prev_q1 = q2
        hs = u % 2
        q3 = OP("ve", (lambda e, hs=hs, rs=rs: e.tensor_scalar(
            stg[hs], wt[:, rs * 2048:(rs + 1) * 2048],
            -MAGIC, None, ALU.add)),
            deps=(q2, st_war[hs]))
        prev_q3 = q3
        rq_war[rs] = q3
        hw = OP("gp", (lambda e, tc=tc, q4=q4, hs=hs: e.dma_start(
            out=hq_dram[tc * 128:(tc + 1) * 128, q4 * 2048:(q4 + 1) * 2048],
            in_=stg[hs])),
            deps=(q3,), dma=True)
        st_war[hs] = hw
        ev_hq[(tc, q4)] = hw
    ev_hq_all = list(ev_hq.values())

    if debug:
        dbg_evs2 = [OP("gp", (lambda e: e.dma_start(out=dbg["d_s"], in_=AR4[:, 0:16384])),
                    deps=(ev_s,), dma=True),
                    OP("gp", (lambda e: e.dma_start(out=dbg["d_shsc"], in_=shsc[:, :])),
                    deps=(ev_shw,), dma=True)]
    else:
        dbg_evs2 = []

    ev_fin_sg = []
    sh_ld = OP("sy", (lambda e: e.dma_start(
        out=sh_row[0:1, :], in_=sh_dram[:, 0:1].rearrange("a b -> b a"))),
        deps=(ev_shw, ev_hq_all, prev_q3), dma=True)

    def hqT_ap(k2):
        if k2 < 16:
            return AR2[:, k2 * 1024:(k2 + 1) * 1024]
        if k2 < 32:
            return AR3[:, (k2 - 16) * 1024:(k2 - 16 + 1) * 1024]
        if k2 < 48:
            return AR0[:, (k2 - 32) * 1024:(k2 - 32 + 1) * 1024]
        if k2 < 61:
            return AR1[:, 2048 + (k2 - 48) * 1024: 2048 + (k2 - 48 + 1) * 1024]
        return wt[:].bitcast(BF16)[:, (k2 - 61) * 1024:(k2 - 61 + 1) * 1024]

    def hqT_war(k2):
        if k2 < 16:
            return (last_gu_mm,)
        if k2 < 32:
            return (ev_h16, prev_q1)
        if k2 < 48:
            return (ev_xqT, last_gu_mm)
        if k2 < 61:
            return (last_gu_mm,)
        return (ev_hq_all,)

    ev_hqT = {}
    for k2 in range(64):
        for th in range(2):
            deps = tuple(ev_hq[(tc, k2 // 16)] for tc in range(th * 4, th * 4 + 4))
            ev_hqT[(k2, th)] = OP("sy", (lambda e, k2=k2, th=th: e.dma_start_transpose(
                hqT_ap(k2)[:, th * 512:(th + 1) * 512],
                hq_dram[th * 512:(th + 1) * 512, k2 * 128:(k2 + 1) * 128])),
                deps=deps + hqT_war(k2), dma=True)

    swcol = OP("ve", (lambda e: e.tensor_scalar(thr[:, 3:4], thr[:, 2:3], 2.0, None, ALU.mult)),
               deps=(thr_p,))
    bs_evs = []
    bprev = None
    for th in range(2):
        bs2 = OP("pe", (lambda e, th=th: e.matmul(
            PB[:, 1536:2048], ones[0:1, :],
            sh_row[0:1, th * 512:(th + 1) * 512], start=True, stop=True)),
            deps=(sh_ld, bprev if bprev is not None else nrm_evs[-1], ev_ones, ev_s))
        bs3 = OP("ve", (lambda e, th=th: e.tensor_scalar(
            bcast[:, th * 512:(th + 1) * 512], PB[:, 1536:2048],
            thr[:, 3:4], None, ALU.mult)),
            deps=(bs2, swcol, last_gu_mm, ev_hq_all, prev_q3)),
        bs3 = bs3[0]
        bprev = bs3
        bs_evs.append(bs3)

    td_war = [last_gu_mm, last_gu_mm]
    sg_war = [None, None]
    pd_free = [prev_q3] * 8
    last_down_mm = None
    for q4o in range(4):
        mm = None
        for k2 in range(64):
            tb = k2 % 2
            ld = OP("sy", (lambda e, k2=k2, q4o=q4o, tb=tb: e.dma_start_transpose(
                AR1[:, tb * 512:(tb + 1) * 512],
                t_dram[2][q4o * 512:(q4o + 1) * 512, k2 * 128:(k2 + 1) * 128])),
                deps=(td_war[tb],), dma=True)

            def d_mms(e, k2=k2, tb=tb):
                rr = None
                for oc in range(4):
                    for th in range(2):
                        rr = e.matmul(bank8(oc * 2 + th),
                                      AR1[:, tb * 512 + oc * 128: tb * 512 + (oc + 1) * 128],
                                      hqT_ap(k2)[:, th * 512:(th + 1) * 512],
                                      start=(k2 == 0), stop=(k2 == 63))
                return rr
            mm = OP("pe", d_mms,
                    deps=(ld, ev_hqT[(k2, 0)], ev_hqT[(k2, 1)], bs_evs)
                    + (tuple(pd_free) if k2 == 0 else ()))
            td_war[tb] = mm
        last_down_mm = mm
        pd_free = []
        prev_y2 = None
        prev_y3 = None
        for oc in range(4):
            for th in range(2):
                o2c = q4o * 4 + oc
                sgslot = th
                y1 = OP("ve", (lambda e, oc=oc, th=th: e.tensor_mul(
                    qb[:, 0:512], bank8(oc * 2 + th), bcast[:, th * 512:(th + 1) * 512])),
                    deps=(mm, prev_y2,
                          (qb4_war, rc_prev, prev_q3) if (q4o == 0 and oc == 0 and th == 0) else None))
                y2 = OP("se", (lambda e: e.activation(qb[:, 512:1024], qb[:, 0:512], AF.Silu)),
                        deps=(y1, prev_y3))
                pad7 = OP("ve", (lambda e: e.tensor_copy(qb[:, 6:7], qb[:, 1023:1024])),
                          deps=(y2,))
                y3 = OP("ve", (lambda e, o2c=o2c, th=th, sgslot=sgslot: e.tensor_mul(
                    AR1[:, 1024 + sgslot * 512: 1024 + (sgslot + 1) * 512],
                    qb[:, 512:1024],
                    AR4[:, o2c * 1024 + th * 512: o2c * 1024 + (th + 1) * 512])),
                    deps=(y2, pad7, sg_war[sgslot],
                          (ev_s, dbg_evs2) if (q4o == 0 and oc == 0) else None))
                prev_y2 = y2
                prev_y3 = y3
                pd_free.append(y1)
                sg = OP("gp", (lambda e, o2c=o2c, th=th, sgslot=sgslot: e.dma_start(
                    out=sg_dram[o2c * 128:(o2c + 1) * 128, th * 512:(th + 1) * 512],
                    in_=AR1[:, 1024 + sgslot * 512: 1024 + (sgslot + 1) * 512])),
                    deps=(y3,), dma=True)
                sg_war[sgslot] = sg
                ev_fin_sg.append(sg)


    ev_out = []
    sgT_war = [ev_fin_sg[-1], ev_fin_sg[-1]]
    xo_war = [last_down_mm, last_down_mm]
    of_war = [ev_fin_sg[-2], ev_fin_sg[-1]]
    for tc in range(8):
        sl = tc % 2
        tr = OP("sy", (lambda e, tc=tc, sl=sl: e.dma_start_transpose(
            AR0[:, sl * 2048:(sl + 1) * 2048],
            sg_dram[:, tc * 128:(tc + 1) * 128])),
            deps=(ev_fin_sg, sgT_war[sl], last_down_mm), dma=True)
        xl = OP("sy", (lambda e, tc=tc, sl=sl: e.dma_start(
            out=AR1f[:, sl * 2048:(sl + 1) * 2048],
            in_=x_own[tc * 128:(tc + 1) * 128, :])),
            deps=(xo_war[sl],) + ((tuple(ev_fin_sg),) if tc < 2 else ()), dma=True)
        ad = OP("ve", (lambda e, sl=sl: e.tensor_add(
            AR1f[:, 4096 + sl * 2048: 4096 + (sl + 1) * 2048],
            AR1f[:, sl * 2048:(sl + 1) * 2048],
            AR0[:, sl * 2048:(sl + 1) * 2048])),
            deps=(tr, xl, of_war[sl]))
        sgT_war[sl] = ad
        xo_war[sl] = ad
        stq = OP("sy", (lambda e, tc=tc, sl=sl: e.dma_start(
            out=out_own[tc * 128:(tc + 1) * 128, :],
            in_=AR1f[:, 4096 + sl * 2048: 4096 + (sl + 1) * 2048])),
            deps=(ad,), dma=True)
        of_war[sl] = stq
        ev_out.append(stq)
    OP("sy", (lambda e: e.nop()), deps=(ev_out,))

    if debug:
        def ddma(nm, src, deps):
            n = src.shape[-1]
            cs = 512 if n % 512 == 0 else n
            return OP("gp", (lambda e: e.dma_start(
                out=dbg[nm].rearrange("a (b c) -> a b c", c=cs),
                in_=src.rearrange("a (b c) -> a b c", c=cs))), deps=deps, dma=True)
        ddma("d_xn", xn_dram, (ev_xn_done,))
        ddma("d_xq", xq_dram, (ev_xq_done,))
        ddma("d_hq", hq_dram, (ev_hq_all,))
        ddma("d_h16", h16_dram, (ev_h16,))
        ddma("d_sgd", sg_dram, (ev_fin_sg,))
        ddma("d_tg", t_dram[0], (tern["tfree"],))
        ddma("d_td", t_dram[2], (tern["tfree"],))

    block = es.enter_context(nc.Block())

    @block.sync
    def _(e):
        p.emit_engine("sy", e)

    @block.gpsimd
    def _(e):
        p.emit_engine("gp", e)

    @block.tensor
    def _(e):
        p.emit_engine("pe", e)

    @block.vector
    def _(e):
        p.emit_engine("ve", e)

    @block.scalar
    def _(e):
        p.emit_engine("se", e)

    es.close()
    return nc


_PERM_SRC = None


def _perm_src():
    global _PERM_SRC
    if _PERM_SRC is None:
        pm = np.empty(2048, np.int64)
        for s_ in range(2048):
            j, rem = divmod(s_, 512)
            r, rem2 = divmod(rem, 128)
            pp, d = divmod(rem2, 64)
            pm[s_] = (2 * j + pp) * 256 + r * 64 + d
        _PERM_SRC = pm
    return _PERM_SRC


_NC_CACHE = None


def kernel(L0_attn_x, L0_attn_w_qkv, L0_attn_b_qkv, L0_attn_w_out, L0_attn_b_out,
           L0_ffn_w_gate, L0_ffn_w_up, L0_ffn_w_down):
    global _NC_CACHE
    x = np.asarray(L0_attn_x, np.float32)
    perm = _perm_src()
    wqkv_p = np.ascontiguousarray(
        np.concatenate([np.asarray(L0_attn_w_qkv, np.float32)[:2048][perm],
                        np.asarray(L0_attn_w_qkv, np.float32)[2048:]], axis=0))
    bqkv_p = np.ascontiguousarray(
        np.concatenate([np.asarray(L0_attn_b_qkv, np.float32)[:2048][perm],
                        np.asarray(L0_attn_b_qkv, np.float32)[2048:]], axis=0))
    wout_p = np.ascontiguousarray(np.asarray(L0_attn_w_out, np.float32)[:, perm])
    bout = np.ascontiguousarray(np.asarray(L0_attn_b_out, np.float32))
    wg = np.ascontiguousarray(np.asarray(L0_ffn_w_gate, np.float32))
    wu = np.ascontiguousarray(np.asarray(L0_ffn_w_up, np.float32))
    wd = np.ascontiguousarray(np.asarray(L0_ffn_w_down, np.float32))

    if _NC_CACHE is None:
        _NC_CACHE = build()
    nc = _NC_CACHE
    xb = [np.ascontiguousarray(x[0]), np.ascontiguousarray(x[1])]
    in_maps = []
    for c in range(8):
        b, t0 = c // 4, (c % 4) * 1024
        in_maps.append({
            "x_batch": xb[b],
            "x_own": np.ascontiguousarray(x[b, t0:t0 + 1024]),
            "w_qkv": wqkv_p, "b_qkv": bqkv_p, "w_out": wout_p, "b_out": bout,
            "w_gate": wg, "w_up": wu, "w_down": wd,
        })
    res = run_bass_kernel_spmd(nc, in_maps, core_ids=list(range(8)))
    out = np.empty((2, 4096, 2048), np.float32)
    for c in range(8):
        b, t0 = c // 4, (c % 4) * 1024
        out[b, t0:t0 + 1024] = res.results[c]["out_own"]
    return out
